# version 09
# baseline (speedup 1.0000x reference)
import contextlib
import math
import numpy as np
import concourse.bass as bass
import concourse.mybir as mybir
from concourse.bass_utils import run_bass_kernel_spmd

F32 = mybir.dt.float32
BF16 = mybir.dt.bfloat16
AF = mybir.ActivationFunctionType
ALU = mybir.AluOpType
AX = mybir.AxisListType

HID = 1024
S = 4096
H = 16
D = 64
CTX = 71
SOFTCAP = 50.0
Q_SCALE = D ** (-0.5) / math.log(2)
K_SCALE = math.log(1.0 + math.e) / math.log(2)
LN_EPS = 1e-5
D_INNER = 1536
D_STATE = 8
NCC = D_INNER // 128
ST = 512
NST = S // ST
NOC = HID // 128
SLOTW = 576
BASE_COUPLING = 0.1

ENGINES = ("pe", "dve", "act", "pool", "sp")


class Buf:
    __slots__ = ("name", "writer", "readers")

    def __init__(self, name=""):
        self.name = name
        self.writer = None
        self.readers = []


class Prog:

    def __init__(self, nc, same_engine_sync=()):
        self.nc = nc
        self.same_engine_sync = set(same_engine_sync)
        self.queues = {e: [] for e in ENGINES}
        self.scounts = {e: 0 for e in ENGINES}
        self.waited = {e: {} for e in ENGINES}
        self.sems = {}

    def _emit_waits(self, engine, deps):
        waits = []
        for skey, cnt in deps.items():
            if cnt <= 0:
                continue
            if skey == engine and engine not in self.same_engine_sync:
                continue
            if self.waited[engine].get(skey, 0) >= cnt:
                continue
            self.waited[engine][skey] = cnt
            waits.append((skey, cnt))
        return waits

    @staticmethod
    def _add_dep(deps, skey, c):
        if deps.get(skey, -1) < c:
            deps[skey] = c

    def op(self, engine, reads=(), writes=(), emit=None, inc=1):
        is_dma = inc == 16
        if is_dma:
            assert writes, "DMA op must declare a written buf"
            skey = "d_" + writes[0].name
        else:
            skey = engine
        deps = {}
        for b in reads:
            if b.writer is not None:
                self._add_dep(deps, *b.writer)
        for b in writes:
            if b.writer is not None:
                self._add_dep(deps, *b.writer)
            for rd in b.readers:
                self._add_dep(deps, *rd)
        waits = self._emit_waits(engine, deps)
        newcount = self.scounts.get(skey, 0) + inc
        self.scounts[skey] = newcount

        def thunk(eng, _w=waits, _e=emit, _i=inc, _sk=skey):
            for (sk, c) in _w:
                eng.wait_ge(self.sems[sk], c)
            ins = _e(eng)
            ins.then_inc(self.sems[_sk], _i)

        self.queues[engine].append(thunk)
        me = (skey, newcount)
        for b in reads:
            b.readers.append(me)
        for b in writes:
            b.writer = me
            b.readers = []
        return newcount

    def dma(self, engine, reads=(), writes=(), emit=None):
        return self.op(engine, reads, writes, emit, inc=16)

    def raw(self, engine, emit):
        self.queues[engine].append(emit)

    def barrier(self):
        snap = dict(self.scounts)
        for e in ENGINES:
            deps = {k: v for k, v in snap.items() if k != e}
            waits = self._emit_waits(e, deps)
            if not waits:
                continue

            def thunk(eng, _w=waits):
                for (sk, c) in _w:
                    eng.wait_ge(self.sems[sk], c)

            self.queues[e].append(thunk)

    def finalize(self):
        nc = self.nc
        stack = contextlib.ExitStack()
        for sk in self.scounts:
            self.sems[sk] = stack.enter_context(nc.semaphore(name=f"s_{sk}"))
        name_map = {"pe": "tensor", "dve": "vector", "act": "scalar",
                    "pool": "gpsimd", "sp": "sync"}
        with nc.Block() as block:
            for e in ENGINES:
                q = self.queues[e]

                def body(eng, _q=q):
                    for f in _q:
                        f(eng)

                getattr(block, name_map[e])(body)
        stack.close()
        return nc


import os


def build_core(phases=4, ablate=99, sub=99, subb=99):
    _acq = int(os.environ.get('ACQ', '4'))
    _natt = int(os.environ.get('NATT', '99'))
    nc = bass.Bass("TRN2", target_bir_lowering=False, debug=False)
    p = Prog(nc, same_engine_sync=tuple(
        x for x in os.environ.get('SAFE_ENGINES', 'act,dve,pool').split(',') if x))
    bufs = {}

    def B(name):
        if name not in bufs:
            bufs[name] = Buf(name)
        return bufs[name]

    def din(name, shape):
        return nc.dram_tensor(name, shape, F32, kind="ExternalInput").ap()

    x_in = din("x_fm", [HID, S])
    wqT = din("wqT", [128, 8, HID])
    wkT = din("wkT", [128, 8, HID])
    wvT = din("wvT", [128, 8, HID])
    wrelT = din("wrelT", [128, 8, HID])
    wpostT = din("wpostT", [128, 8, HID])
    posT = din("posT", [128, 8, CTX])
    pds2 = din("pds2", [128, 1])
    cn_w_d = [din("cn_w1", [128, 8, 1024]), din("cn_w2", [128, 8, 512]),
              din("cn_w3", [128, 4, 256]), din("cn_w4", [128, 2, 1])]
    cn_b_d = [din("cn_b1", [128, 8]), din("cn_b2", [128, 4]),
              din("cn_b3", [128, 2]), din("cn_b4", [1, 1])]
    gc_w_d = [din("gc_w1", [128, 8, 512]), din("gc_w2", [128, 4, 256]),
              din("gc_w3", [128, 2, 1])]
    gc_b_d = [din("gc_b1", [128, 4]), din("gc_b2", [128, 2]), din("gc_b3", [1, 1])]
    ar_in = din("ar", [1, 1])
    winT = din("winT", [128, 8, 3072])
    woutT = din("woutT", [128, 12, HID])
    alog_in = din("alog", [128, NCC, D_STATE])
    bmat_in = din("bmat", [128, NCC, D_STATE])
    cmat_in = din("cmat", [128, NCC, D_STATE])
    lnio_d = {"ig": din("ln_in_g", [128, 8]), "ib": din("ln_in_b", [128, 8]),
              "og": din("ln_out_g", [128, 8]), "ob": din("ln_out_b", [128, 8])}

    y_out = nc.dram_tensor("y_fm", [HID, S], F32, kind="ExternalOutput").ap()
    dbg_out = nc.dram_tensor("dbg", [4, 128, 1136], F32, kind="ExternalOutput").ap()
    _dbg = int(os.environ.get("DBG", "0"))

    xA = nc.dram_tensor("xA", [HID, S], F32, kind="Internal").ap()
    xB = nc.dram_tensor("xB", [HID, S], F32, kind="Internal").ap()
    xC = nc.dram_tensor("xC", [HID, S], F32, kind="Internal").ap()
    brC = nc.dram_tensor("brC", [HID, S], F32, kind="Internal").ap()
    bdsc = nc.dram_tensor("bdsc", [4, H, 2, 2304], F32, kind="Internal").ap()
    gt_d = nc.dram_tensor("gt_d", [D_INNER, S], BF16, kind="Internal").ap()
    tfh_d = nc.dram_tensor("tfh_d", [NST, ST], F32, kind="Internal").ap()
    attb_d = nc.dram_tensor("attb_d", [71, 2048], BF16, kind="Internal").ap()
    statsD = nc.dram_tensor("statsD", [6, 4, S], F32, kind="Internal").ap()

    IDENT = nc.alloc_sbuf_tensor("IDENT", [128, 128], BF16)
    ONESB = nc.alloc_sbuf_tensor("ONESB", [128, 1], BF16)
    ONESF = nc.alloc_sbuf_tensor("ONESF", [1, 128], F32)
    NONESF = nc.alloc_sbuf_tensor("NONESF", [1, 128], F32)

    LNROW = nc.alloc_sbuf_tensor("LNROW", [1, 2, ST], F32)
    STEV = nc.alloc_sbuf_tensor("STEV", [1, 2, ST], F32)
    GX = nc.alloc_sbuf_tensor("GX", [128, 2, 512], F32)
    GR = nc.alloc_sbuf_tensor("GR", [128, 2, 512], F32)
    GO = nc.alloc_sbuf_tensor("GO", [128, 2, 512], F32)
    GB = nc.alloc_sbuf_tensor("GB", [128, 2, 512], BF16)
    GQ = nc.alloc_sbuf_tensor("GQ", [128, 2, 512], BF16)
    PS = nc.alloc_psum_tensor("PS", [128, 4096], F32)

    LN1, LN2, LNI, LNO, LN4 = 0, 1, 2, 3, 4
    PSB = [B(f"psb{i}") for i in range(8)]

    pe_state = {"base": -1, "partial": False}

    def sep_mm(e):
        e.matmul(PS[96:97, 7 * 512:7 * 512 + 1], ONESB[:, :], ONESB[:, :],
                 start=True, stop=True, tile_position=(0, 96),
                 skip_group_check=True)

    def pe_guard(first_base, first_k, last_base, last_k, emit_fn):
        need = (pe_state["partial"] and first_k < 128
                and pe_state["base"] != first_base)
        pe_state["base"] = last_base
        pe_state["partial"] = last_k < 128
        if not need:
            return emit_fn

        def wrapped(e, _f=emit_fn):
            sep_mm(e)
            return _f(e)
        return wrapped

    def bank(i, w=1):
        return PS[:, i * 512:(i + w) * 512]

    def emit_stats_chunk(xbf_ap, xsq_ap, rd_bufs, ln, st, oc, last=7):
        p.op("pe", reads=rd_bufs + [B("ONESB")], writes=[PSB[4]],
             emit=pe_guard(0, 128, 0, 128,
                           lambda e: e.matmul(PS[0:1, 4 * 512:4 * 512 + 512],
                                              ONESB[:, :], xbf_ap,
                                              start=(oc == 0), stop=(oc == last),
                                              skip_group_check=True)))
        p.op("pe", reads=rd_bufs + [B("ONESB")], writes=[PSB[5]],
             emit=pe_guard(0, 128, 0, 128,
                           lambda e: e.matmul(PS[0:1, 5 * 512:5 * 512 + 512],
                                              ONESB[:, :], xsq_ap,
                                              start=(oc == 0), stop=(oc == last),
                                              skip_group_check=True)))
        if oc == last:
            p.op("dve", reads=[PSB[4]], writes=[B("STEV")],
                 emit=lambda e: e.tensor_copy(STEV[:, 0, :],
                                              PS[0:1, 4 * 512:4 * 512 + 512]))
            p.op("dve", reads=[PSB[5], B("STEV")], writes=[B("STEV")],
                 emit=lambda e: e.tensor_copy(STEV[:, 1, :],
                                              PS[0:1, 5 * 512:5 * 512 + 512]))
            p.dma("sp", reads=[B("STEV")], writes=[B("statsD")],
                  emit=lambda e, ln=ln, st=st: e.dma_start(
                      out=statsD[ln, 0:2, st * ST:(st + 1) * ST].unsqueeze(0),
                      in_=STEV[:, :, :]))

    def finalize_ln(ln):
        with nc.sbuf_tensor("FIN", [1, 2, S], F32) as FIN:
            bf_ = B(f"FIN{ln}")
            p.dma("sp", reads=[B("statsD")], writes=[bf_],
                  emit=lambda e: e.dma_start(out=FIN[:, :, :].squeeze(0),
                                             in_=statsD[ln, 0:2, :]))
            p.op("dve", reads=[bf_], writes=[bf_],
                 emit=lambda e: e.tensor_single_scalar(FIN[:, 0, :], FIN[:, 0, :],
                                                       1.0 / HID, ALU.mult))
            p.op("dve", reads=[bf_], writes=[bf_],
                 emit=lambda e: e.tensor_single_scalar(FIN[:, 1, :], FIN[:, 1, :],
                                                       1.0 / HID, ALU.mult))
            p.op("dve", reads=[bf_], writes=[bf_],
                 emit=lambda e: e.scalar_tensor_tensor(FIN[:, 1, :], FIN[:, 0, :],
                                                       1.0, FIN[:, 1, :],
                                                       ALU.mult, ALU.subtract))
            raise AssertionError("placeholder")

    def finalize_ln(ln):
        with nc.sbuf_tensor(f"FIN{ln}", [1, 3, S], F32) as FIN:
            bf_ = B(f"FIN{ln}")
            p.dma("sp", reads=[B("statsD")], writes=[bf_],
                  emit=lambda e: e.dma_start(out=FIN[:, 0:2, :],
                                             in_=statsD[ln, 0:2, :].unsqueeze(0)))
            p.op("dve", reads=[bf_], writes=[bf_],
                 emit=lambda e: e.tensor_single_scalar(FIN[:, 0, :], FIN[:, 0, :],
                                                       1.0 / HID, ALU.mult))
            p.op("dve", reads=[bf_], writes=[bf_],
                 emit=lambda e: e.tensor_tensor(FIN[:, 2, :], FIN[:, 0, :],
                                                FIN[:, 0, :], ALU.mult))
            p.op("dve", reads=[bf_], writes=[bf_],
                 emit=lambda e: e.scalar_tensor_tensor(FIN[:, 2, :], FIN[:, 1, :],
                                                       1.0 / HID, FIN[:, 2, :],
                                                       ALU.mult, ALU.subtract))
            p.op("dve", reads=[bf_], writes=[bf_],
                 emit=lambda e: e.tensor_single_scalar(FIN[:, 2, :], FIN[:, 2, :],
                                                       LN_EPS, ALU.add))
            p.op("act", reads=[bf_], writes=[bf_],
                 emit=lambda e: e.activation(FIN[:, 2, :], FIN[:, 2, :], AF.Sqrt))
            p.op("dve", reads=[bf_], writes=[bf_],
                 emit=lambda e: e.reciprocal(FIN[:, 2, :], FIN[:, 2, :]))
            p.op("dve", reads=[bf_], writes=[bf_],
                 emit=lambda e: e.tensor_tensor(FIN[:, 0, :], FIN[:, 0, :],
                                                FIN[:, 2, :], ALU.mult))
            p.dma("sp", reads=[bf_], writes=[B("statsD")],
                  emit=lambda e: e.dma_start(
                      out=statsD[ln, 2, :].unsqueeze(0), in_=FIN[:, 0, :]))
            p.dma("sp", reads=[bf_], writes=[B("statsD")],
                  emit=lambda e: e.dma_start(
                      out=statsD[ln, 3, :].unsqueeze(0), in_=FIN[:, 2, :]))
        p.barrier()

    def emit_ln_bcast(ln, st, bank_r=2, bank_m=3):
        p.dma("sp", reads=[B("statsD")], writes=[B("LNROW")],
              emit=lambda e, ln=ln, st=st: e.dma_start(
                  out=LNROW[:, :, :],
                  in_=statsD[ln, 2:4, st * ST:(st + 1) * ST].unsqueeze(0)))
        p.op("pe", reads=[B("LNROW"), B("ONESF")], writes=[PSB[bank_r]],
             emit=pe_guard(0, 1, 0, 1,
                           lambda e: e.matmul(bank(bank_r), ONESF[:, :],
                                              LNROW[0:1, 1, :],
                                              start=True, stop=True)))
        p.op("pe", reads=[B("LNROW"), B("ONESF")], writes=[PSB[bank_m]],
             emit=pe_guard(0, 1, 0, 1,
                           lambda e: e.matmul(bank(bank_m), ONESF[:, :],
                                              LNROW[0:1, 0, :],
                                              start=True, stop=True)))

    def proj_gemm(dst_emit, lhsW, lhsbuf, rhsT, rhsbuf, n_oc, k_ic=8,
                  banks=(0, 1), n=512):
        for oc in range(n_oc):
            bk = banks[oc % 2]
            def mk(oc=oc, bk=bk):
                def em(e):
                    ins = None
                    for ic in range(k_ic):
                        ins = e.matmul(PS[:, bk * 512:bk * 512 + n],
                                       lhsW[:, ic, oc * 128:(oc + 1) * 128],
                                       rhsT(ic), start=(ic == 0),
                                       stop=(ic == k_ic - 1))
                    return ins
                return em
            p.op("pe", reads=[lhsbuf, rhsbuf], writes=[PSB[bk]],
                 emit=pe_guard(0, 128, 0, 128, mk()))
            dst_emit(oc, bk)

    p.op("dve", writes=[B("ZEROT")], emit=lambda e: e.memset(ZEROT[:, :], 0.0))
    p.op("dve", writes=[B("ONESF")], emit=lambda e: e.memset(ONESF[:, :], 1.0))
    p.op("dve", writes=[B("NONESF")], emit=lambda e: e.memset(NONESF[:, :], -1.0))
    p.op("pool", writes=[B("ONESB")], emit=lambda e: e.memset(ONESB[:, :], 1.0))

    scopeA = contextlib.ExitStack()

    def sa(nm, shape, dt):
        return scopeA.enter_context(nc.sbuf_tensor(nm, shape, dt))

    WQ = sa("WQ", [128, 8, HID], BF16)
    WK = sa("WK", [128, 8, HID], BF16)
    WV = sa("WV", [128, 8, HID], BF16)
    WPOST = sa("WPOST", [128, 8, HID], BF16)
    RELK = sa("RELK", [128, 8, CTX], BF16)
    QSC = sa("QSC", [128, 1], F32)

    scope0 = contextlib.ExitStack()
    IOTP = scope0.enter_context(nc.sbuf_tensor("IOTP", [128, 128], F32))
    IOTF = scope0.enter_context(nc.sbuf_tensor("IOTF", [128, 128], F32))
    POSB = scope0.enter_context(nc.sbuf_tensor("POSB", [128, 8, CTX], BF16))
    WREL = scope0.enter_context(nc.sbuf_tensor("WREL", [128, 8, HID], BF16))
    PDS = scope0.enter_context(nc.sbuf_tensor("PDS", [128, 4], F32))
    ZEROT = scope0.enter_context(nc.sbuf_tensor("ZEROT", [128, 512], F32))

    p.op("pool", writes=[B("IOTP")],
         emit=lambda e: e.iota(IOTP[:, :], pattern=[[0, 128]], channel_multiplier=1,
                               allow_small_or_imprecise_dtypes=True))
    p.op("pool", writes=[B("IOTF")],
         emit=lambda e: e.iota(IOTF[:, :], pattern=[[1, 128]], channel_multiplier=0,
                               allow_small_or_imprecise_dtypes=True))
    p.op("dve", reads=[B("IOTP"), B("IOTF")], writes=[B("IDENT")],
         emit=lambda e: e.tensor_tensor(IDENT[:, :], IOTP[:, :], IOTF[:, :],
                                        ALU.is_equal))

    for nm, dst, src in (("WQ", WQ, wqT), ("WK", WK, wkT), ("WV", WV, wvT),
                         ("WPOST", WPOST, wpostT)):
        p.dma("pool", writes=[B(nm)],
              emit=lambda e, d=dst, s=src: e.dma_start(out=d[:, :, :], in_=s))
    p.dma("pool", writes=[B("POSB")],
          emit=lambda e: e.dma_start(out=POSB[:, :, :], in_=posT))
    p.dma("pool", writes=[B("WREL")],
          emit=lambda e: e.dma_start(out=WREL[:, :, :], in_=wrelT))
    for oc in range(8):
        def mk(oc=oc):
            def em(e):
                ins = None
                for ic in range(8):
                    ins = e.matmul(PS[:, 6 * 512:6 * 512 + CTX],
                                   WREL[:, ic, oc * 128:(oc + 1) * 128],
                                   POSB[:, ic, :], start=(ic == 0), stop=(ic == 7))
                return ins
            return em
        p.op("pe", reads=[B("WREL"), B("POSB")], writes=[PSB[6]],
             emit=pe_guard(0, 128, 0, 128, mk()))
        p.op("dve", reads=[PSB[6]], writes=[B("RELK")],
             emit=lambda e, oc=oc: e.tensor_copy(RELK[:, oc, :],
                                                 PS[:, 6 * 512:6 * 512 + CTX]))

    p.dma("sp", writes=[B("PDS")],
          emit=lambda e: e.dma_start(out=PDS[:, 0:1], in_=pds2))
    p.op("act", reads=[B("PDS")], writes=[B("PDSa")],
         emit=lambda e: e.activation(PDS[:, 1:2], PDS[:, 0:1], AF.Abs))
    p.op("act", reads=[B("PDSa")], writes=[B("PDSe")],
         emit=lambda e: e.activation(PDS[:, 2:3], PDS[:, 1:2], AF.Exp, scale=-1.0))
    p.op("act", reads=[B("PDSe")], writes=[B("PDSl")],
         emit=lambda e: e.activation(PDS[:, 3:4], PDS[:, 2:3], AF.Ln, bias=1.0))
    p.op("dve", reads=[B("PDS")], writes=[B("PDSr")],
         emit=lambda e: e.tensor_single_scalar(PDS[:, 0:1], PDS[:, 0:1], 0.0,
                                               ALU.max))
    p.op("dve", reads=[B("PDSr"), B("PDSl")], writes=[B("QSC")],
         emit=lambda e: e.tensor_tensor(QSC[:, :], PDS[:, 0:1], PDS[:, 3:4],
                                        ALU.add))
    p.op("dve", reads=[B("QSC")], writes=[B("QSC")],
         emit=lambda e: e.tensor_single_scalar(QSC[:, :], QSC[:, :], Q_SCALE,
                                               ALU.mult))

    TOTBD = 4 * 2 * H * 2304
    nfull = TOTBD // 65536
    for i in range(nfull + 1):
        n0 = i * 65536
        n1 = min((i + 1) * 65536, TOTBD)
        if n1 <= n0:
            continue
        rows = (n1 - n0) // 512
        p.dma("sp", reads=[B("ZEROT")],
              writes=[B(f"bdz{i}")] + [B(f"bdsc{q}") for q in range(4)],
              emit=lambda e, n0=n0, n1=n1, rows=rows: e.dma_start(
                  out=bdsc.rearrange("a b c d -> (a b c d)")[n0:n1]
                  .rearrange("(r c) -> r c", c=512),
                  in_=ZEROT[0:rows, 0:512]))

    for st in range(NST):
        for oc in range(NOC):
            pg = oc % 2
            bx, bb, bq = B(f"GX{pg}"), B(f"GB{pg}"), B(f"GQ{pg}")
            p.dma("sp", writes=[bx],
                  emit=lambda e, st=st, oc=oc, pg=pg: e.dma_start(
                      out=GX[:, pg, :],
                      in_=x_in[oc * 128:(oc + 1) * 128, st * ST:(st + 1) * ST]))
            p.op("act", reads=[bx], writes=[bb],
                 emit=lambda e, pg=pg: e.activation(GB[:, pg, :], GX[:, pg, :],
                                                    AF.Copy))
            p.op("act", reads=[bb], writes=[bq],
                 emit=lambda e, pg=pg: e.activation(GQ[:, pg, :], GB[:, pg, :],
                                                    AF.Square))
            emit_stats_chunk(GB[:, pg, :], GQ[:, pg, :], [bb, bq], LN1, st, oc)
    finalize_ln(LN1)
    scope0.close()

    XLN = sa("XLN", [128, 8, 512], BF16)
    QR = sa("QR", [128, 2, 8, 512], BF16)
    KSL = sa("KSL", [128, 2, 8, SLOTW], BF16)
    VSL = sa("VSL", [128, 2, 8, SLOTW], BF16)
    V2I = sa("V2I", [128, H, 512], BF16)
    BDS = sa("BDS", [128, H, 2, CTX], F32)
    SSB = sa("SSB", [128, 2, 16 * CTX], F32)
    ATB = sa("ATB", [128, 2, 16 * CTX], BF16)
    ATT = sa("ATT", [71, 2, 1024], BF16)
    ATTB = sa("ATTB", [128, 2, 1024], BF16)
    AOUT = sa("AOUT", [128, 8, 512], BF16)
    RSUM = sa("RSUM", [128, 2, 16], F32)

    def kv_overlap(i, W, nm):
        r = i % 2
        rp = (i - 1) % 2
        if i > 0:
            p.dma("sp", reads=[B(f"{nm}{r}")], writes=[B(f"{nm}{rp}")],
                  emit=lambda e: e.dma_start(out=W[:, rp, :, 543:576],
                                             in_=W[:, r, :, 31:64]))
            p.dma("sp", reads=[B(f"{nm}{rp}")], writes=[B(f"{nm}{r}")],
                  emit=lambda e: e.dma_start(out=W[:, r, :, 0:31],
                                             in_=W[:, rp, :, 512:543]))
        else:
            p.op("pool", writes=[B(f"{nm}{r}")],
                 emit=lambda e: e.memset(W[:, r, :, 0:31], 0.0))
        if i == NST - 1:
            p.op("pool", reads=[B(f"{nm}{r}")], writes=[B(f"{nm}{r}")],
                 emit=lambda e: e.memset(W[:, r, :, 543:576], 0.0))

    for i in range(NST + 1):
        if i < NST:
            st = i
            r = st % 2
            xp = st % 2
            emit_ln_bcast(LN1, st, 2, 3)
            for oc in range(NOC):
                cp = oc % 2
                bx = B(f"GX{cp}")
                p.dma("sp", writes=[bx],
                      emit=lambda e, st=st, oc=oc, cp=cp: e.dma_start(
                          out=GX[:, cp, :],
                          in_=x_in[oc * 128:(oc + 1) * 128, st * ST:(st + 1) * ST]))
                p.op("dve", reads=[bx, PSB[2]], writes=[bx],
                     emit=lambda e, cp=cp: e.tensor_tensor(
                         GX[:, cp, :], GX[:, cp, :], bank(2), ALU.mult))
                p.op("dve", reads=[bx, PSB[3]], writes=[B("XLN")],
                     emit=lambda e, oc=oc, cp=cp: e.tensor_tensor(
                         XLN[:, oc, :], GX[:, cp, :], bank(3), ALU.subtract))

            def qdst(oc, bk, xp=xp):
                p.op("act", reads=[PSB[bk], B("QSC")], writes=[B(f"QR{xp}")],
                     emit=lambda e, oc=oc, bk=bk: e.activation(
                         QR[:, xp, oc, :], bank(bk), AF.Copy, scale=QSC[:, 0:1]))
            proj_gemm(qdst, WQ, B("WQ"), lambda ic: XLN[:, ic, :], B("XLN"), 8)

            def kdst(oc, bk, r=r):
                p.op("act", reads=[PSB[bk]], writes=[B(f"K{r}")],
                     emit=lambda e, oc=oc, bk=bk: e.activation(
                         KSL[:, r, oc, 31:543], bank(bk), AF.Copy, scale=K_SCALE))
            proj_gemm(kdst, WK, B("WK"), lambda ic: XLN[:, ic, :], B("XLN"), 8)

            def vdst(oc, bk, r=r):
                p.op("act", reads=[PSB[bk]], writes=[B(f"V{r}")],
                     emit=lambda e, oc=oc, bk=bk: e.activation(
                         VSL[:, r, oc, 31:543], bank(bk), AF.Copy))
            proj_gemm(vdst, WV, B("WV"), lambda ic: XLN[:, ic, :], B("XLN"), 8)
            kv_overlap(st, KSL, "K")
            kv_overlap(st, VSL, "V")

        if i < NST and ablate < 1:
            continue
        if i >= 1 and ablate >= 2 and i <= _natt:
            j = i - 1
            rj = j % 2
            xpj = j % 2
            _vord = [h for h in range(H) if h % 2 == 0] + \
                    [h for h in range(H) if h % 2 == 1]
            for h in (_vord if ablate >= 3 else []):
                hr = (h % 2) * 64
                hc = h // 2
                def mkv2(hr=hr, hc=hc, rj=rj):
                    def em(e):
                        ins = None
                        for m in range(8):
                            ins = e.transpose(
                                bank(6).bitcast(BF16)[:, m * 64:(m + 1) * 64],
                                VSL[hr:hr + 64, rj, hc, m * 64:m * 64 + 128],
                                IDENT[hr:hr + 64, hr:hr + 64])
                        return ins
                    return em
                p.op("pe", reads=[B(f"V{rj}"), B("IDENT")], writes=[PSB[6]],
                     emit=pe_guard(hr, 64, hr, 64, mkv2()))
                p.op("act", reads=[PSB[6]], writes=[B(f"V2I{h}")],
                     emit=lambda e, h=h: e.activation(
                         V2I[:, h, :], bank(6).bitcast(BF16)[:, 0:512], AF.Copy))
            for gh in range(2 if ablate >= 4 else 0):
                for h in ([x for x in range(H) if x % 2 == 0] +
                          [x for x in range(H) if x % 2 == 1]):
                    hr = (h % 2) * 64
                    hc = h // 2
                    def mkbd(h=h, hr=hr, hc=hc, xpj=xpj, gh=gh):
                        def em(e):
                            ins = None
                            for gl in range(2):
                                g = gh * 2 + gl
                                ins = e.matmul(
                                    PS[:, 6 * 512 + gl * 128:
                                       6 * 512 + gl * 128 + CTX],
                                    QR[hr:hr + 64, xpj, hc,
                                       g * 128:(g + 1) * 128],
                                    RELK[hr:hr + 64, hc, :],
                                    start=True, stop=True)
                            return ins
                        return em
                    p.op("pe", reads=[B(f"QR{xpj}"), B("RELK")], writes=[PSB[6]],
                         emit=pe_guard(hr, 64, hr, 64, mkbd()))
                    p.op("act", reads=[PSB[6]], writes=[B("BDS")],
                         emit=lambda e, h=h: e.activation(
                             BDS[:, h, :, :],
                             bank(6).rearrange("p (g c) -> p g c",
                                               g=4)[:, 0:2, 0:CTX], AF.Copy))
                for q in range(4 if sub >= 1 else 0):
                    p.dma("sp", reads=[B("BDS")], writes=[B(f"bdsc{q}")],
                          emit=lambda e, q=q: e.dma_start(
                              out=bdsc[q, :, :, :].rearrange(
                                  "h g (c l) -> c h g l", l=72)[:, :, :, 0:71],
                              in_=BDS[q * 32:(q + 1) * 32, :, :, :]))
                for q in range(4 if sub >= 1 else 0):
                    p.dma("sp", reads=[B(f"bdsc{q}")], writes=[B("BDS")],
                          emit=lambda e, q=q: e.dma_start(
                              out=BDS[q * 32:(q + 1) * 32, :, :, :],
                              in_=bdsc[q, :, :, 0:2272].rearrange(
                                  "h g (c l) -> c h g l", l=71)))
                for gl in range(2 if sub >= 2 else 0):
                    g = gh * 2 + gl
                    for half in range(2):
                        sb = 2 if half == 0 else 4
                        h0 = half * 8
                        for hh in ([x for x in range(8) if (h0 + x) % 2 == 0] +
                                   [x for x in range(8) if (h0 + x) % 2 == 1]):
                            h = h0 + hh
                            hr = (h % 2) * 64
                            hc = h // 2
                            def mkac(g=g, hr=hr, hc=hc, hh=hh, sb=sb,
                                     xpj=xpj, rj=rj):
                                def em(e):
                                    ins = None
                                    for q in range(_acq):
                                        tp = ({"tile_position": (hr, 96)}
                                              if q == 3 else {})
                                        win = g * 128 + q * 32
                                        ins = e.matmul(
                                            PS[q * 32:(q + 1) * 32,
                                               sb * 512 + hh * 128:
                                               sb * 512 + hh * 128 + CTX],
                                            QR[hr:hr + 64, xpj, hc,
                                               g * 128 + q * 32:
                                               g * 128 + q * 32 + 32],
                                            KSL[hr:hr + 64, rj, hc,
                                                win:win + CTX],
                                            start=True, stop=True, **tp)
                                    return ins
                                return em
                            p.op("pe", reads=[B(f"QR{xpj}"), B(f"K{rj}")],
                                 writes=[PSB[sb], PSB[sb + 1]],
                                 emit=pe_guard(hr, 64, hr, 64, mkac()))
                        if subb < 1:
                            continue
                        gp = g % 2
                        p.op("dve", reads=[PSB[sb], PSB[sb + 1], B("BDS")],
                             writes=[B(f"SSB{gp}")],
                             emit=lambda e, sb=sb, h0=h0, gl=gl, gp=gp: e.tensor_tensor(
                                 SSB[:, gp, h0 * CTX:(h0 + 8) * CTX].rearrange(
                                     "p (h c) -> p h c", h=8),
                                 bank(sb, 2).rearrange(
                                     "p (h c) -> p h c", h=8)[:, :, 0:CTX],
                                 BDS[:, h0:h0 + 8, gl, :], ALU.add))
                        if _dbg and j == 0 and g == 0 and half == 1:
                            p.dma("sp", reads=[B("SSB")], writes=[B("dbg0")],
                                  emit=lambda e: e.dma_start(
                                      out=dbg_out[0], in_=SSB[:, :]))
                        if subb >= 2:
                            p.op("act", reads=[B(f"SSB{gp}")], writes=[B(f"SSB{gp}")],
                                 emit=lambda e, h0=h0, gp=gp: e.activation(
                                     SSB[:, gp, h0 * CTX:(h0 + 8) * CTX],
                                     SSB[:, gp, h0 * CTX:(h0 + 8) * CTX],
                                     AF.Tanh, scale=1.0 / SOFTCAP))
                        if subb >= 2:
                            p.op("act", reads=[B(f"SSB{gp}")], writes=[B(f"SSB{gp}")],
                                 emit=lambda e, h0=h0, gp=gp: e.activation(
                                     SSB[:, gp, h0 * CTX:(h0 + 8) * CTX],
                                     SSB[:, gp, h0 * CTX:(h0 + 8) * CTX],
                                     AF.Exp, scale=SOFTCAP))
                        if subb < 3:
                            continue
                        p.op("dve", reads=[B(f"SSB{gp}")], writes=[B(f"RSUM{gp}")],
                             emit=lambda e, h0=h0, gp=gp: e.reduce_sum(
                                 RSUM[:, gp, h0:h0 + 8].unsqueeze(2),
                                 SSB[:, gp, h0 * CTX:(h0 + 8) * CTX].rearrange(
                                     "p (h c) -> p h c", h=8), axis=AX.X))
                    if _dbg and j == 0 and g == 0:
                        p.dma("sp", reads=[B("SSB")], writes=[B("dbg1")],
                              emit=lambda e: e.dma_start(
                                  out=dbg_out[1], in_=SSB[:, :]))
                        p.dma("sp", reads=[B("RSUM")], writes=[B("dbg2")],
                              emit=lambda e: e.dma_start(
                                  out=dbg_out[2][:, 0:16], in_=RSUM[:, :]))
                        p.dma("sp", reads=[B("BDS")], writes=[B("dbg3")],
                              emit=lambda e: e.dma_start(
                                  out=dbg_out[3][:, 0:16 * CTX]
                                  .rearrange("p (h c) -> p h c", h=16),
                                  in_=BDS[:, :, 0, :]))
                    if subb < 4:
                        continue
                    gp = g % 2
                    p.op("dve", reads=[B(f"RSUM{gp}")], writes=[B(f"RSUM{gp}")],
                         emit=lambda e, gp=gp: e.reciprocal(RSUM[:, gp, :],
                                                            RSUM[:, gp, :]))
                    for h in range(H):
                        p.op("dve", reads=[B(f"SSB{gp}"), B(f"RSUM{gp}")],
                             writes=[B(f"ATB{gp}")],
                             emit=lambda e, h=h, gp=gp: e.tensor_scalar(
                                 ATB[:, gp, h * CTX:(h + 1) * CTX],
                                 SSB[:, gp, h * CTX:(h + 1) * CTX],
                                 RSUM[:, gp, h:h + 1], None, ALU.mult))
                    for half in range(2 if sub >= 3 else 0):
                        def mktr(half=half, gp=gp):
                            def em(e):
                                ins = None
                                for hh in range(8):
                                    h = half * 8 + hh
                                    ins = e.transpose(
                                        bank(7).bitcast(BF16)[0:71,
                                            hh * 128:hh * 128 + 128],
                                        ATB[:, gp, h * CTX:(h + 1) * CTX],
                                        IDENT[:, :])
                                return ins
                            return em
                        p.op("pe", reads=[B(f"ATB{gp}"), B("IDENT")], writes=[PSB[7]],
                             emit=pe_guard(0, 128, 0, 128, mktr()))
                        p.op("act", reads=[PSB[7]], writes=[B(f"ATT{half}")],
                             emit=lambda e, half=half, gp=gp: e.activation(
                                 ATT[:, half, :],
                                 bank(7).bitcast(BF16)[0:71, 0:1024], AF.Copy))
                        def mktrs(half=half, gp=gp):
                            def em(e):
                                ins = None
                                for hh in range(8):
                                    h = half * 8 + hh
                                    e.transpose(
                                        bank(6).bitcast(BF16)[32:64,
                                            hh * 128:hh * 128 + 128],
                                        ATB[:, gp, h * CTX:h * CTX + 32],
                                        IDENT[:, :])
                                    ins = e.transpose(
                                        bank(6).bitcast(BF16)[64:103,
                                            hh * 128:hh * 128 + 128],
                                        ATB[:, gp, h * CTX + 32:h * CTX + CTX],
                                        IDENT[:, :])
                                return ins
                            return em
                        p.op("pe", reads=[B(f"ATB{gp}"), B("IDENT")], writes=[PSB[6]],
                             emit=pe_guard(0, 128, 0, 128, mktrs()))
                        p.op("act", reads=[PSB[6]], writes=[B(f"ATTB{half}")],
                             emit=lambda e, half=half, gp=gp: e.activation(
                                 ATTB[32:64, half, :],
                                 bank(6).bitcast(BF16)[32:64, 0:1024], AF.Copy))
                        p.op("act", reads=[PSB[6]], writes=[B(f"ATTB{half}")],
                             emit=lambda e, half=half, gp=gp: e.activation(
                                 ATTB[64:103, half, :],
                                 bank(6).bitcast(BF16)[64:103, 0:1024], AF.Copy))
                    if sub < 3:
                        continue
                    if sub < 4:
                        continue
                    def mkav(g=g):
                        def em(e):
                            ins = None
                            for h in range(H):
                                half = h // 8
                                hh = h % 8
                                pr = h // 2
                                for q in (0, 2):
                                    sloc = g * 128 + q * 32
                                    m = sloc // 64
                                    ins = e.matmul(
                                        PS[(h % 2) * 64:(h % 2) * 64 + 64,
                                           2 * 512 + pr * 128 + q * 32:
                                           2 * 512 + pr * 128 + q * 32 + 32],
                                        V2I[0:CTX, h, m * 64:(m + 1) * 64],
                                        ATT[0:71, half,
                                            hh * 128 + q * 32:hh * 128 + q * 32 + 32],
                                        start=True, stop=True)
                            sep_mm(e)
                            for h in range(H):
                                half = h // 8
                                hh = h % 8
                                pr = h // 2
                                for q in (1, 3):
                                    sloc = g * 128 + q * 32
                                    m = sloc // 64
                                    ins = e.matmul(
                                        PS[(h % 2) * 64:(h % 2) * 64 + 64,
                                           2 * 512 + pr * 128 + q * 32:
                                           2 * 512 + pr * 128 + q * 32 + 32],
                                        V2I[32:64, h, m * 64:(m + 1) * 64],
                                        ATTB[32:64, half,
                                             hh * 128 + q * 32:hh * 128 + q * 32 + 32],
                                        start=True, stop=True,
                                        skip_group_check=True)
                            sep_mm(e)
                            for h in range(H):
                                half = h // 8
                                hh = h % 8
                                pr = h // 2
                                for q in (1, 3):
                                    sloc = g * 128 + q * 32
                                    m = sloc // 64
                                    ins = e.matmul(
                                        PS[(h % 2) * 64:(h % 2) * 64 + 64,
                                           4 * 512 + pr * 64 + (q // 2) * 32:
                                           4 * 512 + pr * 64 + (q // 2) * 32 + 32],
                                        V2I[64:103, h, m * 64:(m + 1) * 64],
                                        ATTB[64:103, half,
                                             hh * 128 + q * 32:hh * 128 + q * 32 + 32],
                                        start=True, stop=True,
                                        skip_group_check=True)
                            return ins
                        return em
                    p.op("pe",
                         reads=[B("ATT0"), B("ATT1"),
                                B("ATTB0"), B("ATTB1")] +
                               [B(f"V2I{h}") for h in range(H)],
                         writes=[PSB[2], PSB[3], PSB[4]],
                         emit=pe_guard(0, 71, 64, 39, mkav()))
                    p.op("dve", reads=[PSB[2], PSB[3]], writes=[B("AOUT")],
                         emit=lambda e, g=g: e.tensor_copy(
                             AOUT[:, :, g * 128:(g + 1) * 128],
                             bank(2, 2).rearrange("p (r t) -> p r t", r=8)))
                    p.op("dve", reads=[PSB[4], B("AOUT")], writes=[B("AOUT")],
                         emit=lambda e, g=g: e.tensor_tensor(
                             AOUT[:, :, g * 128:(g + 1) * 128].rearrange(
                                 "p r (q t) -> p r q t", q=4)[:, :, 1::2, :],
                             bank(4).rearrange("p (r q t) -> p r q t", r=8, q=2),
                             AOUT[:, :, g * 128:(g + 1) * 128].rearrange(
                                 "p r (q t) -> p r q t", q=4)[:, :, 1::2, :],
                             ALU.add))
            if ablate < 5:
                continue
            def postdst(oc, bk, j=j):
                cp = oc % 2
                bx = B(f"GR{cp}")
                p.dma("sp", writes=[bx],
                      emit=lambda e, j=j, oc=oc, cp=cp: e.dma_start(
                          out=GR[:, cp, :],
                          in_=x_in[oc * 128:(oc + 1) * 128, j * ST:(j + 1) * ST]))
                b1 = B(f"GO{cp}")
                p.op("dve", reads=[PSB[bk], bx], writes=[b1],
                     emit=lambda e, bk=bk, cp=cp: e.tensor_tensor(
                         GO[:, cp, :], bank(bk), GR[:, cp, :], ALU.add))
                p.dma("sp", reads=[b1], writes=[B("xA")],
                      emit=lambda e, j=j, oc=oc, cp=cp: e.dma_start(
                          out=xA[oc * 128:(oc + 1) * 128, j * ST:(j + 1) * ST],
                          in_=GO[:, cp, :]))
                bb = B(f"GB{cp}")
                bq = B(f"GQ{cp}")
                p.op("act", reads=[b1], writes=[bb],
                     emit=lambda e, cp=cp: e.activation(GB[:, cp, :],
                                                        GO[:, cp, :], AF.Copy))
                p.op("act", reads=[bb], writes=[bq],
                     emit=lambda e, cp=cp: e.activation(GQ[:, cp, :],
                                                        GB[:, cp, :], AF.Square))
                emit_stats_chunk(GB[:, cp, :], GQ[:, cp, :], [bb, bq], LN2, j, oc)
            proj_gemm(postdst, WPOST, B("WPOST"), lambda ic: AOUT[:, ic, :],
                      B("AOUT"), 8)

    p.barrier()
    scopeA.close()

    def ffn_phase(tag, x_src, x_dst, ln_src, ln_stats_out, dst_buf_name):
        SH = S // 2
        finalize_ln(ln_src)
        sc = contextlib.ExitStack()

        def sf(nm, shape, dt):
            return sc.enter_context(nc.sbuf_tensor(tag + nm, shape, dt))

        OUT = sf("OUT", [128, 8, SH], F32)
        OUTB = sf("OUTB", [128, 8, 512], BF16)
        HB1 = sf("HB1", [128, 8, 512], BF16)
        HB2 = sf("HB2", [128, 4, 512], BF16)
        HB3 = sf("HB3", [128, 2, 512], BF16)
        CW1 = sf("CW1", [128, 8, 1024], BF16)
        CW2 = sf("CW2", [128, 8, 512], BF16)
        CW3 = sf("CW3", [128, 4, 256], BF16)
        CW4 = sf("CW4", [128, 2, 1], BF16)
        CBX = sf("CBX", [128, 16], F32)
        GW1 = sf("GW1", [128, 8, 512], BF16)
        GW2 = sf("GW2", [128, 4, 256], BF16)
        GW3 = sf("GW3", [128, 2, 1], BF16)
        GBX = sf("GBX", [128, 8], F32)
        TFH = sf("TFH", [1, 1, 512], F32)
        CPL = sf("CPL", [1, 1, 2048], F32)
        GACC = sf("GACC", [128, 8, 8], F32)
        MEANT = sf("MEANT", [128, 8], F32)
        MEANB = sf("MEANB", [128, 8], BF16)
        GH = sf("GH", [128, 8], BF16)
        SML = sf("SML", [1, 8], F32)
        TMPU = sf("TMPU", [128, 2048], F32)
        TMPV = sf("TMPV", [128, 2048], F32)

        TB = lambda n: B(tag + n)

        for nm, dst, srcd in (("CW1", CW1, cn_w_d[0]), ("CW2", CW2, cn_w_d[1]),
                              ("CW3", CW3, cn_w_d[2]), ("CW4", CW4, cn_w_d[3]),
                              ("GW1", GW1, gc_w_d[0]), ("GW2", GW2, gc_w_d[1]),
                              ("GW3", GW3, gc_w_d[2])):
            p.dma("pool", writes=[TB(nm)],
                  emit=lambda e, d=dst, s=srcd: e.dma_start(out=d[:, :, :], in_=s))
        p.dma("sp", writes=[TB("CBX1")],
              emit=lambda e: e.dma_start(out=CBX[:, 0:8], in_=cn_b_d[0]))
        p.dma("sp", writes=[TB("CBX2")],
              emit=lambda e: e.dma_start(out=CBX[:, 8:12], in_=cn_b_d[1]))
        p.dma("sp", writes=[TB("CBX3")],
              emit=lambda e: e.dma_start(out=CBX[:, 12:14], in_=cn_b_d[2]))
        p.dma("sp", writes=[TB("SMLb4")],
              emit=lambda e: e.dma_start(out=SML[0:1, 0:1], in_=cn_b_d[3]))
        p.dma("sp", writes=[TB("GBX1")],
              emit=lambda e: e.dma_start(out=GBX[:, 0:4], in_=gc_b_d[0]))
        p.dma("sp", writes=[TB("GBX2")],
              emit=lambda e: e.dma_start(out=GBX[:, 4:6], in_=gc_b_d[1]))
        p.dma("sp", writes=[TB("SMLg3")],
              emit=lambda e: e.dma_start(out=SML[0:1, 1:2], in_=gc_b_d[2]))
        p.dma("sp", writes=[TB("SMLar")],
              emit=lambda e: e.dma_start(out=SML[0:1, 3:4], in_=ar_in))
        p.op("dve", reads=[TB("SMLb4")], writes=[TB("SMLb4")],
             emit=lambda e: e.tensor_single_scalar(SML[0:1, 0:1], SML[0:1, 0:1],
                                                   0.5, ALU.mult))
        p.op("dve", reads=[TB("SMLg3")], writes=[TB("SMLg3")],
             emit=lambda e: e.tensor_single_scalar(SML[0:1, 1:2], SML[0:1, 1:2],
                                                   0.5, ALU.mult))
        p.op("dve", reads=[TB("SMLar")], writes=[TB("SMLar3")],
             emit=lambda e: e.tensor_single_scalar(SML[0:1, 4:5], SML[0:1, 3:4],
                                                   0.3, ALU.mult))

        for st in range(NST):
            emit_ln_bcast(ln_src, st, 2, 3)
            for oc in range(NOC):
                cp = oc % 2
                bx = B(f"GX{cp}")
                bo = B(f"GO{cp}")
                bb = B(f"GB{cp}")
                p.dma("sp", writes=[bx],
                      emit=lambda e, st=st, oc=oc, cp=cp: e.dma_start(
                          out=GX[:, cp, :],
                          in_=x_src[oc * 128:(oc + 1) * 128,
                                    st * ST:(st + 1) * ST]))
                p.op("dve", reads=[bx, PSB[2]], writes=[bx],
                     emit=lambda e, cp=cp: e.tensor_tensor(
                         GX[:, cp, :], GX[:, cp, :], bank(2), ALU.mult))
                p.op("dve", reads=[bx, PSB[3]], writes=[bo],
                     emit=lambda e, cp=cp: e.tensor_tensor(
                         GO[:, cp, :], GX[:, cp, :], bank(3), ALU.subtract))
                p.op("act", reads=[bo], writes=[bb, TB("GACC")],
                     emit=lambda e, st=st, oc=oc, cp=cp: e.activation(
                         GB[:, cp, :], GO[:, cp, :], AF.Copy,
                         accum_out=GACC[:, oc, st:st + 1]))

        p.op("dve", reads=[TB("GACC")], writes=[TB("MEANT")],
             emit=lambda e: e.reduce_sum(MEANT[:, :].unsqueeze(2),
                                         GACC[:, :, :], axis=AX.X))
        p.op("dve", reads=[TB("MEANT")], writes=[TB("MEANT")],
             emit=lambda e: e.tensor_single_scalar(MEANT[:, :], MEANT[:, :],
                                                   1.0 / S, ALU.mult))
        p.op("act", reads=[TB("MEANT")], writes=[TB("MEANB")],
             emit=lambda e: e.activation(MEANB[:, :], MEANT[:, :], AF.Copy))

        def mkg1():
            def em(e):
                ins = None
                for oc in range(4):
                    for ic in range(8):
                        ins = e.matmul(PS[:, oc:oc + 1],
                                       GW1[:, ic, oc * 128:(oc + 1) * 128],
                                       MEANB[:, ic:ic + 1], start=(ic == 0),
                                       stop=(ic == 7), skip_group_check=True)
                return ins
            return em
        p.op("pe", reads=[TB("GW1"), TB("MEANB")], writes=[PSB[0]],
             emit=pe_guard(0, 128, 0, 128, mkg1()))
        for oc in range(4):
            p.op("act", reads=[PSB[0], TB("GBX1")], writes=[TB("GH")],
                 emit=lambda e, oc=oc: e.activation(
                     GH[:, oc:oc + 1], PS[:, oc:oc + 1], AF.Gelu,
                     bias=GBX[:, oc:oc + 1]))

        def mkg2():
            def em(e):
                ins = None
                for oc in range(2):
                    for ic in range(4):
                        ins = e.matmul(PS[:, 512 + oc:513 + oc],
                                       GW2[:, ic, oc * 128:(oc + 1) * 128],
                                       GH[:, ic:ic + 1], start=(ic == 0),
                                       stop=(ic == 3), skip_group_check=True)
                return ins
            return em
        p.op("pe", reads=[TB("GW2"), TB("GH")], writes=[PSB[1]],
             emit=pe_guard(0, 128, 0, 128, mkg2()))
        for oc in range(2):
            p.op("act", reads=[PSB[1], TB("GBX2")], writes=[TB("GH2")],
                 emit=lambda e, oc=oc: e.activation(
                     GH[:, 4 + oc:5 + oc], PS[:, 512 + oc:513 + oc], AF.Gelu,
                     bias=GBX[:, 4 + oc:5 + oc]))

        def mkg3():
            def em(e):
                ins = None
                for ic in range(2):
                    ins = e.matmul(PS[0:1, 1024:1025], GW3[:, ic, 0:1],
                                   GH[:, 4 + ic:5 + ic], start=(ic == 0),
                                   stop=(ic == 1), skip_group_check=True)
                return ins
            return em
        p.op("pe", reads=[TB("GW3"), TB("GH2")], writes=[PSB[2]],
             emit=pe_guard(0, 128, 0, 128, mkg3()))
        p.op("act", reads=[PSB[2], TB("SMLg3")], writes=[TB("SMLghh")],
             emit=lambda e: e.activation(SML[0:1, 2:3], PS[0:1, 1024:1025],
                                         AF.Tanh, scale=0.5,
                                         bias=SML[0:1, 1:2]))
        p.op("dve", reads=[TB("SMLghh"), TB("SMLar")], writes=[TB("SMLgt")],
             emit=lambda e: e.tensor_tensor(SML[0:1, 5:6], SML[0:1, 3:4],
                                            SML[0:1, 2:3], ALU.mult))
        p.op("dve", reads=[TB("SMLgt")], writes=[TB("SMLgt")],
             emit=lambda e: e.tensor_scalar(SML[0:1, 5:6], SML[0:1, 5:6],
                                            0.7, BASE_COUPLING, ALU.mult,
                                            ALU.add))

        def cn_for_st(st, lsl):
            def d1(oc, bk):
                p.op("act", reads=[PSB[bk], TB("CBX1")], writes=[TB("HB1")],
                     emit=lambda e, oc=oc, bk=bk: e.activation(
                         HB1[:, oc, :], bank(bk), AF.Gelu, bias=CBX[:, oc:oc + 1]))
            proj_gemm(d1, CW1, TB("CW1"), lambda ic: OUTB[:, ic, :], TB("OUTB"), 8)

            def d2(oc, bk):
                p.op("act", reads=[PSB[bk], TB("CBX2")], writes=[TB("HB2")],
                     emit=lambda e, oc=oc, bk=bk: e.activation(
                         HB2[:, oc, :], bank(bk), AF.Gelu,
                         bias=CBX[:, 8 + oc:9 + oc]))
            proj_gemm(d2, CW2, TB("CW2"), lambda ic: HB1[:, ic, :], TB("HB1"), 4)

            def d3(oc, bk):
                p.op("act", reads=[PSB[bk], TB("CBX3")], writes=[TB("HB3")],
                     emit=lambda e, oc=oc, bk=bk: e.activation(
                         HB3[:, oc, :], bank(bk), AF.Gelu,
                         bias=CBX[:, 12 + oc:13 + oc]))
            proj_gemm(d3, CW3, TB("CW3"), lambda ic: HB2[:, ic, :], TB("HB2"),
                      2, k_ic=4)

            def mk4():
                def em(e):
                    ins = None
                    for ic in range(2):
                        ins = e.matmul(PS[0:1, 0:512], CW4[:, ic, 0:1],
                                       HB3[:, ic, :], start=(ic == 0),
                                       stop=(ic == 1))
                    return ins
                return em
            p.op("pe", reads=[TB("CW4"), TB("HB3")], writes=[PSB[0]],
                 emit=pe_guard(0, 128, 0, 128, mk4()))
            p.op("act", reads=[PSB[0], TB("SMLb4")], writes=[TB("TFHs")],
                 emit=lambda e: e.activation(
                     TFH[0:1, 0, :], PS[0:1, 0:512], AF.Tanh,
                     scale=0.5, bias=SML[0:1, 0:1]))
            p.dma("sp", reads=[TB("TFHs")], writes=[TB("tfh_d")],
                  emit=lambda e, st=st: e.dma_start(
                      out=tfh_d[st, :].unsqueeze(0), in_=TFH[0:1, 0, :]))

        for hf in range(2):
            sts = list(range(hf * 4, hf * 4 + 4))
            for st in sts:
                lsl = slice((st - hf * 4) * ST, (st - hf * 4 + 1) * ST)
                emit_ln_bcast(ln_src, st, 2, 3)
                for oc in range(NOC):
                    cp = oc % 2
                    bx = B(f"GX{cp}")
                    p.dma("sp", writes=[bx],
                          emit=lambda e, st=st, oc=oc, cp=cp: e.dma_start(
                              out=GX[:, cp, :],
                              in_=x_src[oc * 128:(oc + 1) * 128,
                                        st * ST:(st + 1) * ST]))
                    p.op("dve", reads=[bx, PSB[2]], writes=[bx],
                         emit=lambda e, cp=cp: e.tensor_tensor(
                             GX[:, cp, :], GX[:, cp, :], bank(2), ALU.mult))
                    p.op("dve", reads=[bx, PSB[3]], writes=[TB("OUT")],
                         emit=lambda e, oc=oc, cp=cp, lsl=lsl: e.tensor_tensor(
                             OUT[:, oc, lsl], GX[:, cp, :], bank(3),
                             ALU.subtract))
                    p.op("act", reads=[TB("OUT")], writes=[TB("OUTB")],
                         emit=lambda e, oc=oc, lsl=lsl: e.activation(
                             OUTB[:, oc, :], OUT[:, oc, lsl], AF.Copy))
                cn_for_st(st, lsl)
            for k in range(3):
                p.dma("sp", reads=[TB("tfh_d")], writes=[TB("TFHs")],
                      emit=lambda e, hf=hf: e.dma_start(
                          out=CPL[0:1, 0, :],
                          in_=tfh_d[hf * 4:hf * 4 + 4, :].rearrange(
                              "a b -> (a b)").unsqueeze(0)))
                p.op("dve", reads=[TB("TFHs"), TB("SMLar3"), TB("SMLgt")],
                     writes=[TB("CPL")],
                     emit=lambda e: e.tensor_scalar(
                         CPL[0:1, 0, :], CPL[0:1, 0, :], SML[0:1, 4:5],
                         SML[0:1, 5:6], ALU.mult, ALU.add))
                for qb in range(4):
                    p.op("pe", reads=[TB("CPL"), B("ONESF")], writes=[PSB[2 + qb]],
                         emit=pe_guard(0, 1, 0, 1,
                                       lambda e, qb=qb: e.matmul(
                                           bank(2 + qb), ONESF[:, :],
                                           CPL[0:1, 0, qb * 512:(qb + 1) * 512],
                                           start=True, stop=True)))
                hsl = slice(0, SH)
                for pi, (a, bq) in enumerate([(0, 4), (1, 5), (2, 6), (3, 7)]):
                    alu1 = ALU.add if a < 2 else ALU.subtract
                    alu2 = ALU.subtract if a < 2 else ALU.add
                    p.op("dve", reads=[PSB[2], PSB[3], PSB[4], PSB[5], TB("OUT")],
                         writes=[TB("TMPU")],
                         emit=lambda e, bq=bq: e.tensor_tensor(
                             TMPU[:, :], bank(2, 4), OUT[:, bq, :], ALU.mult))
                    p.op("dve", reads=[PSB[2], PSB[3], PSB[4], PSB[5], TB("OUT")],
                         writes=[TB("TMPV")],
                         emit=lambda e, a=a: e.tensor_tensor(
                             TMPV[:, :], bank(2, 4), OUT[:, a, :], ALU.mult))
                    p.op("dve", reads=[TB("TMPU"), TB("OUT")], writes=[TB("OUT")],
                         emit=lambda e, a=a, alu1=alu1: e.tensor_tensor(
                             OUT[:, a, :], OUT[:, a, :], TMPU[:, :], alu1))
                    p.op("dve", reads=[TB("TMPV"), TB("OUT")], writes=[TB("OUT")],
                         emit=lambda e, bq=bq, alu2=alu2: e.tensor_tensor(
                             OUT[:, bq, :], OUT[:, bq, :], TMPV[:, :], alu2))
                for st in sts:
                    lsl = slice((st - hf * 4) * ST, (st - hf * 4 + 1) * ST)
                    if k < 2:
                        for oc in range(NOC):
                            p.op("act", reads=[TB("OUT")], writes=[TB("OUTB")],
                                 emit=lambda e, oc=oc, lsl=lsl: e.activation(
                                     OUTB[:, oc, :], OUT[:, oc, lsl], AF.Copy))
                        cn_for_st(st, lsl)
                    else:
                        for oc in range(NOC):
                            cp = oc % 2
                            bx = B(f"GR{cp}")
                            p.dma("sp", writes=[bx],
                                  emit=lambda e, st=st, oc=oc, cp=cp: e.dma_start(
                                      out=GR[:, cp, :],
                                      in_=x_src[oc * 128:(oc + 1) * 128,
                                                st * ST:(st + 1) * ST]))
                            bo = B(f"GO{cp}")
                            p.op("dve", reads=[bx, TB("OUT")], writes=[bo],
                                 emit=lambda e, oc=oc, cp=cp, lsl=lsl: e.tensor_tensor(
                                     GO[:, cp, :], OUT[:, oc, lsl],
                                     GR[:, cp, :], ALU.add))
                            p.dma("sp", reads=[bo], writes=[B(dst_buf_name)],
                                  emit=lambda e, st=st, oc=oc, cp=cp: e.dma_start(
                                      out=x_dst[oc * 128:(oc + 1) * 128,
                                                st * ST:(st + 1) * ST],
                                      in_=GO[:, cp, :]))
                            if ln_stats_out is not None:
                                bb = B(f"GB{cp}")
                                bqq = B(f"GQ{cp}")
                                p.op("act", reads=[bo], writes=[bb],
                                     emit=lambda e, cp=cp: e.activation(
                                         GB[:, cp, :], GO[:, cp, :], AF.Copy))
                                p.op("act", reads=[bb], writes=[bqq],
                                     emit=lambda e, cp=cp: e.activation(
                                         GQ[:, cp, :], GB[:, cp, :], AF.Square))
                                emit_stats_chunk(GB[:, cp, :], GQ[:, cp, :],
                                                 [bb, bqq], ln_stats_out, st, oc)
        p.barrier()
        sc.close()

    def attnb_phase():
        finalize_ln(LNI)
        scC = contextlib.ExitStack()

        def sC(nm, shape, dt):
            return scC.enter_context(nc.sbuf_tensor("C" + nm, shape, dt))

        XSSM = sC("XSSM", [128, NCC, S], BF16)
        LNIO = sC("LNIO", [128, 4, 8], F32)
        ACOL = sC("ACOL", [128, 96], F32)
        BCC = sC("BCC", [128, 96], F32)
        ABC = sC("ABC", [128, 2, NCC, D_STATE], F32)

        for i, (nm, srcd) in enumerate((("lnig", lnio_d["ig"]), ("lnib", lnio_d["ib"]),
                                        ("lnog", lnio_d["og"]), ("lnob", lnio_d["ob"]))):
            p.dma("sp", writes=[B("LNIO")],
                  emit=lambda e, i=i, s=srcd: e.dma_start(out=LNIO[:, i, :], in_=s))
        p.dma("sp", writes=[B("ABCa")],
              emit=lambda e: e.dma_start(out=ABC[:, 0, :, :], in_=alog_in))
        p.dma("sp", writes=[B("ABCb")],
              emit=lambda e: e.dma_start(out=ABC[:, 1, :, :], in_=bmat_in))
        p.op("dve", reads=[B("ABCa")], writes=[B("ACOL")],
             emit=lambda e: e.tensor_scalar(
                 ACOL[:, :], ABC[:, 0, :, :].rearrange("p a b -> p (a b)"),
                 0.0, -5.0, ALU.min, ALU.max))
        p.op("act", reads=[B("ACOL")], writes=[B("ACOL")],
             emit=lambda e: e.activation(ACOL[:, :], ACOL[:, :], AF.Exp))
        p.op("dve", reads=[B("ACOL")], writes=[B("ACOL")],
             emit=lambda e: e.tensor_scalar(ACOL[:, :], ACOL[:, :],
                                            2.0, 0.01, ALU.min, ALU.max))
        p.op("dve", reads=[B("ACOL")], writes=[B("ACOL")],
             emit=lambda e: e.tensor_single_scalar(ACOL[:, :], ACOL[:, :],
                                                   -0.9, ALU.mult))
        p.dma("sp", reads=[B("ACOL")], writes=[B("ABCa")],
              emit=lambda e: e.dma_start(out=ABC[:, 0, :, :], in_=cmat_in))
        p.op("dve", reads=[B("ABCa"), B("ABCb")], writes=[B("BCC")],
             emit=lambda e: e.tensor_tensor(
                 BCC[:, :], ABC[:, 0, :, :].rearrange("p a b -> p (a b)"),
                 ABC[:, 1, :, :].rearrange("p a b -> p (a b)"), ALU.mult))
        p.op("dve", reads=[B("BCC")], writes=[B("BCC")],
             emit=lambda e: e.tensor_single_scalar(BCC[:, :], BCC[:, :],
                                                   0.1, ALU.mult))

        scC1 = contextlib.ExitStack()
        WINB = scC1.enter_context(nc.sbuf_tensor("WINB", [128, 8, 3072], BF16))
        XNB = scC1.enter_context(nc.sbuf_tensor("XNB", [128, 8, 512], BF16))
        GTB = scC1.enter_context(nc.sbuf_tensor("GTB", [128, 2, 512], BF16))
        p.dma("pool", writes=[B("WINB")],
              emit=lambda e: e.dma_start(out=WINB[:, :, :], in_=winT))
        for st in range(NST):
            emit_ln_bcast(LNI, st, 2, 3)
            for oc in range(NOC):
                cp = oc % 2
                bx = B(f"GX{cp}")
                p.dma("sp", writes=[bx],
                      emit=lambda e, st=st, oc=oc, cp=cp: e.dma_start(
                          out=GX[:, cp, :],
                          in_=xB[oc * 128:(oc + 1) * 128, st * ST:(st + 1) * ST]))
                p.op("dve", reads=[bx, PSB[2]], writes=[bx],
                     emit=lambda e, cp=cp: e.tensor_tensor(
                         GX[:, cp, :], GX[:, cp, :], bank(2), ALU.mult))
                p.op("dve", reads=[bx, PSB[3]], writes=[bx],
                     emit=lambda e, cp=cp: e.tensor_tensor(
                         GX[:, cp, :], GX[:, cp, :], bank(3), ALU.subtract))
                p.op("dve", reads=[bx, B("LNIO")], writes=[B("XNB")],
                     emit=lambda e, oc=oc, cp=cp: e.tensor_scalar(
                         XNB[:, oc, :], GX[:, cp, :],
                         LNIO[:, 0, oc:oc + 1], LNIO[:, 1, oc:oc + 1],
                         ALU.mult, ALU.add))
            for ocp in range(24):
                bk = ocp % 2
                def mkwin(ocp=ocp, bk=bk):
                    def em(e):
                        ins = None
                        for ic in range(8):
                            ins = e.matmul(PS[:, bk * 512:bk * 512 + 512],
                                           WINB[:, ic, ocp * 128:(ocp + 1) * 128],
                                           XNB[:, ic, :], start=(ic == 0),
                                           stop=(ic == 7))
                        return ins
                    return em
                p.op("pe", reads=[B("WINB"), B("XNB")], writes=[PSB[bk]],
                     emit=pe_guard(0, 128, 0, 128, mkwin()))
                if ocp < NCC:
                    p.op("dve", reads=[PSB[bk]], writes=[B(f"XSSM{ocp}")],
                         emit=lambda e, st=st, ocp=ocp, bk=bk: e.tensor_scalar(
                             XSSM[:, ocp, st * ST:(st + 1) * ST], bank(bk),
                             5.0, -5.0, ALU.min, ALU.max))
                else:
                    cc = ocp - NCC
                    gp = cc % 2
                    p.op("dve", reads=[PSB[bk]], writes=[B(f"GTB{gp}")],
                         emit=lambda e, bk=bk, gp=gp: e.tensor_scalar(
                             GTB[:, gp, :], bank(bk), 5.0, -5.0,
                             ALU.min, ALU.max))
                    p.dma("sp", reads=[B(f"GTB{gp}")], writes=[B("gt_d")],
                          emit=lambda e, st=st, cc=cc, gp=gp: e.dma_start(
                              out=gt_d[cc * 128:(cc + 1) * 128,
                                       st * ST:(st + 1) * ST],
                              in_=GTB[:, gp, :]))
        p.barrier()
        scC1.close()

        scC2 = contextlib.ExitStack()
        BRT = scC2.enter_context(nc.sbuf_tensor("CBRT", [128, 2, 512], F32))
        scS = contextlib.ExitStack()
        XF = scS.enter_context(nc.sbuf_tensor("CXF", [128, S], F32))
        HT = scS.enter_context(nc.sbuf_tensor("CHT", [128, 2, S], F32))
        YC = scS.enter_context(nc.sbuf_tensor("CYC", [128, S], F32))
        YT = scS.enter_context(nc.sbuf_tensor("CYT", [128, S], F32))
        for cc in range(NCC):
            p.op("act", reads=[B(f"XSSM{cc}")], writes=[B("CXF")],
                 emit=lambda e, cc=cc: e.activation(XF[:, :], XSSM[:, cc, :],
                                                    AF.Copy))
            for s in range(D_STATE):
                lane = cc * D_STATE + s
                hp = s % 2
                p.op("dve", reads=[B("CXF"), B("ACOL")], writes=[B(f"CHT{hp}")],
                     emit=lambda e, lane=lane, hp=hp: e.tensor_tensor_scan(
                         HT[:, hp, :], ACOL[:, lane:lane + 1].broadcast_to([128, S]),
                         XF[:, :], 0.0, ALU.mult, ALU.add))
                p.op("act", reads=[B(f"CHT{hp}"), B("BCC")], writes=[B("CYT")],
                     emit=lambda e, lane=lane, hp=hp: e.activation(
                         YT[:, :], HT[:, hp, :], AF.Copy,
                         scale=BCC[:, lane:lane + 1]))
                if s == 0:
                    p.op("pool", reads=[B("CYT")], writes=[B("CYC")],
                         emit=lambda e: e.tensor_copy(YC[:, :], YT[:, :]))
                else:
                    p.op("pool", reads=[B("CYT"), B("CYC")], writes=[B("CYC")],
                         emit=lambda e: e.tensor_tensor(YC[:, :], YC[:, :],
                                                        YT[:, :], ALU.add))
            p.op("dve", reads=[B("CYC")], writes=[B("CYC")],
                 emit=lambda e: e.tensor_scalar(YC[:, :], YC[:, :], 5.0, -5.0,
                                                ALU.min, ALU.max))
            p.dma("pool", writes=[B("CXF")],
                  emit=lambda e, cc=cc: e.dma_start(
                      out=XF[:, :], in_=gt_d[cc * 128:(cc + 1) * 128, :]))
            p.op("act", reads=[B("CXF")], writes=[B("CXF")],
                 emit=lambda e: e.activation(XF[:, :], XF[:, :], AF.Tanh))
            p.op("pool", reads=[B("CXF"), B("CYC")], writes=[B(f"XSSM{cc}")],
                 emit=lambda e, cc=cc: e.tensor_tensor(
                     XSSM[:, cc, :], YC[:, :], XF[:, :], ALU.mult))
        p.barrier()
        scS.close()
        WOUTB = scC2.enter_context(nc.sbuf_tensor("WOUTB", [128, NCC, HID], BF16))
        p.dma("pool", writes=[B("WOUTB")],
              emit=lambda e: e.dma_start(out=WOUTB[:, :, :], in_=woutT))
        for st in range(NST):
            for oc in range(NOC):
                bk = oc % 2
                def mkwo(st=st, oc=oc, bk=bk):
                    def em(e):
                        ins = None
                        for ic in range(NCC):
                            ins = e.matmul(PS[:, bk * 512:bk * 512 + 512],
                                           WOUTB[:, ic, oc * 128:(oc + 1) * 128],
                                           XSSM[:, ic, st * ST:(st + 1) * ST],
                                           start=(ic == 0), stop=(ic == NCC - 1))
                        return ins
                    return em
                p.op("pe",
                     reads=[B("WOUTB")] + [B(f"XSSM{ic}") for ic in range(NCC)],
                     writes=[PSB[bk]], emit=pe_guard(0, 128, 0, 128, mkwo()))
                cp = oc % 2
                p.op("dve", reads=[PSB[bk]], writes=[B(f"CBRT{cp}")],
                     emit=lambda e, bk=bk, cp=cp: e.tensor_scalar(
                         BRT[:, cp, :], bank(bk), 5.0, -5.0, ALU.min, ALU.max))
                p.dma("sp", reads=[B(f"CBRT{cp}")], writes=[B("brC")],
                      emit=lambda e, st=st, oc=oc, cp=cp: e.dma_start(
                          out=brC[oc * 128:(oc + 1) * 128, st * ST:(st + 1) * ST],
                          in_=BRT[:, cp, :]))
                bb = B(f"GB{cp}")
                bqq = B(f"GQ{cp}")
                p.op("act", reads=[B(f"CBRT{cp}")], writes=[bb],
                     emit=lambda e, cp=cp: e.activation(GB[:, cp, :],
                                                        BRT[:, cp, :], AF.Copy))
                p.op("act", reads=[bb], writes=[bqq],
                     emit=lambda e, cp=cp: e.activation(GQ[:, cp, :],
                                                        GB[:, cp, :], AF.Square))
                emit_stats_chunk(GB[:, cp, :], GQ[:, cp, :], [bb, bqq],
                                 LNO, st, oc)
        p.barrier()
        scC2.close()

        finalize_ln(LNO)
        for st in range(NST):
            emit_ln_bcast(LNO, st, 2, 3)
            for oc in range(NOC):
                cp = oc % 2
                bx = B(f"GX{cp}")
                br = B(f"GR{cp}")
                bo = B(f"GO{cp}")
                p.dma("sp", writes=[bx],
                      emit=lambda e, st=st, oc=oc, cp=cp: e.dma_start(
                          out=GX[:, cp, :],
                          in_=brC[oc * 128:(oc + 1) * 128, st * ST:(st + 1) * ST]))
                p.dma("sp", writes=[br],
                      emit=lambda e, st=st, oc=oc, cp=cp: e.dma_start(
                          out=GR[:, cp, :],
                          in_=xB[oc * 128:(oc + 1) * 128, st * ST:(st + 1) * ST]))
                p.op("dve", reads=[bx, PSB[2]], writes=[bx],
                     emit=lambda e, cp=cp: e.tensor_tensor(
                         GX[:, cp, :], GX[:, cp, :], bank(2), ALU.mult))
                p.op("dve", reads=[bx, PSB[3]], writes=[bx],
                     emit=lambda e, cp=cp: e.tensor_tensor(
                         GX[:, cp, :], GX[:, cp, :], bank(3), ALU.subtract))
                p.op("dve", reads=[bx, B("LNIO")], writes=[bo],
                     emit=lambda e, oc=oc, cp=cp: e.tensor_scalar(
                         GO[:, cp, :], GX[:, cp, :],
                         LNIO[:, 2, oc:oc + 1], LNIO[:, 3, oc:oc + 1],
                         ALU.mult, ALU.add))
                p.op("dve", reads=[bo, br], writes=[bo],
                     emit=lambda e, cp=cp: e.tensor_tensor(
                         GO[:, cp, :], GO[:, cp, :], GR[:, cp, :], ALU.add))
                p.dma("sp", reads=[bo], writes=[B("xC")],
                      emit=lambda e, st=st, oc=oc, cp=cp: e.dma_start(
                          out=xC[oc * 128:(oc + 1) * 128, st * ST:(st + 1) * ST],
                          in_=GO[:, cp, :]))
                bb = B(f"GB{cp}")
                bqq = B(f"GQ{cp}")
                p.op("act", reads=[bo], writes=[bb],
                     emit=lambda e, cp=cp: e.activation(GB[:, cp, :],
                                                        GO[:, cp, :], AF.Copy))
                p.op("act", reads=[bb], writes=[bqq],
                     emit=lambda e, cp=cp: e.activation(GQ[:, cp, :],
                                                        GB[:, cp, :], AF.Square))
                emit_stats_chunk(GB[:, cp, :], GQ[:, cp, :], [bb, bqq],
                                 LN4, st, oc)
        p.barrier()
        scC.close()

    if phases >= 2:
        ffn_phase("B", xA, xB, LN2, LNI, "xB")
    if phases >= 3:
        attnb_phase()
    if phases >= 4:
        ffn_phase("D", xC, y_out, LN4, None, "y")

    if phases < 4:
        srcmap = {1: xA, 2: xB, 3: xC}
        xs = srcmap[phases]
        for st2 in range(NST):
            for oc in range(NOC):
                cp = oc % 2
                p.dma("sp", reads=[B("xA"), B("xB"), B("xC")], writes=[B(f"GX{cp}")],
                      emit=lambda e, st2=st2, oc=oc, cp=cp, xs=xs: e.dma_start(
                          out=GX[:, cp, :], in_=xs[oc * 128:(oc + 1) * 128,
                                                   st2 * ST:(st2 + 1) * ST]))
                p.dma("sp", reads=[B(f"GX{cp}")], writes=[B("y")],
                      emit=lambda e, st2=st2, oc=oc, cp=cp: e.dma_start(
                          out=y_out[oc * 128:(oc + 1) * 128,
                                    st2 * ST:(st2 + 1) * ST], in_=GX[:, cp, :]))

    fin = p.scounts.get("d_y", 0)
    p.raw("sp", lambda e, _c=fin: e.wait_ge(p.sems["d_y"], _c))
    p.finalize()
    return nc



def _chunk_lhsT(w):
    K, M = w.shape
    return np.ascontiguousarray(w.reshape(K // 128, 128, M).transpose(1, 0, 2))


def _prep_shared(inp):
    f = np.float32

    def arr(k):
        return np.asarray(inp[k], f)

    d = {}
    d["wqT"] = _chunk_lhsT(np.ascontiguousarray(arr("L0_attn_a_w_q").T))
    d["wkT"] = _chunk_lhsT(np.ascontiguousarray(arr("L0_attn_a_w_k").T))
    d["wvT"] = _chunk_lhsT(np.ascontiguousarray(arr("L0_attn_a_w_v").T))
    d["wrelT"] = _chunk_lhsT(np.ascontiguousarray(arr("L0_attn_a_w_rel").T))
    d["wpostT"] = _chunk_lhsT(np.ascontiguousarray(arr("L0_attn_a_w_post").T))
    d["posT"] = _chunk_lhsT(np.ascontiguousarray(
        arr("L0_attn_a_position_embeddings").T))
    d["pds2"] = np.ascontiguousarray(
        np.tile(arr("L0_attn_a_per_dim_scale"), 2).reshape(128, 1))
    for k in range(4):
        d[f"cn_w{k + 1}"] = _chunk_lhsT(arr(f"L0_ffn_cn_w{k + 1}"))
        b = arr(f"L0_ffn_cn_b{k + 1}")
        d[f"cn_b{k + 1}"] = (b.reshape(1, 1) if b.size == 1
                             else np.ascontiguousarray(b.reshape(-1, 128).T))
    for k in range(3):
        d[f"gc_w{k + 1}"] = _chunk_lhsT(arr(f"L0_ffn_gc_w{k + 1}"))
        b = arr(f"L0_ffn_gc_b{k + 1}")
        d[f"gc_b{k + 1}"] = (b.reshape(1, 1) if b.size == 1
                             else np.ascontiguousarray(b.reshape(-1, 128).T))
    d["ar"] = arr("L0_ffn_adaptive_range").reshape(1, 1)
    d["winT"] = _chunk_lhsT(np.ascontiguousarray(arr("L0_attn_b_W_in").T))
    d["woutT"] = _chunk_lhsT(np.ascontiguousarray(arr("L0_attn_b_W_out").T))
    for nm, key in (("alog", "L0_attn_b_A_log"), ("bmat", "L0_attn_b_B"),
                    ("cmat", "L0_attn_b_C")):
        d[nm] = np.ascontiguousarray(
            arr(key).reshape(NCC, 128, D_STATE).transpose(1, 0, 2))
    for nm, key in (("ln_in_g", "L0_attn_b_ln_in_g"),
                    ("ln_in_b", "L0_attn_b_ln_in_b"),
                    ("ln_out_g", "L0_attn_b_ln_out_g"),
                    ("ln_out_b", "L0_attn_b_ln_out_b")):
        d[nm] = np.ascontiguousarray(arr(key).reshape(8, 128).T)
    return d


_PHASES = 4
_ABLATE = 99
_SUB = 99
_SUBB = 99


def kernel(**inputs):
    shared = _prep_shared(inputs)
    x = np.asarray(inputs["L0_attn_a_hidden_states"], np.float32)
    nc = build_core(phases=_PHASES, ablate=_ABLATE, sub=_SUB, subb=_SUBB)
    in_maps = []
    for b in range(8):
        m = dict(shared)
        m["x_fm"] = np.ascontiguousarray(x[b].T)
        in_maps.append(m)
    res = run_bass_kernel_spmd(nc, in_maps, core_ids=list(range(8)))
    out = np.stack([np.asarray(r["y_fm"]).T for r in res.results], axis=0)
    return out.astype(np.float32)


# version 10 (final)
# speedup vs baseline: 1.0875x; 1.0875x over previous
import contextlib
import math
import numpy as np
import concourse.bass as bass
import concourse.mybir as mybir
from concourse.bass_utils import run_bass_kernel_spmd

F32 = mybir.dt.float32
BF16 = mybir.dt.bfloat16
AF = mybir.ActivationFunctionType
ALU = mybir.AluOpType
AX = mybir.AxisListType

HID = 1024
S = 4096
H = 16
D = 64
CTX = 71
SOFTCAP = 50.0
Q_SCALE = D ** (-0.5) / math.log(2)
K_SCALE = math.log(1.0 + math.e) / math.log(2)
LN_EPS = 1e-5
D_INNER = 1536
D_STATE = 8
NCC = D_INNER // 128
ST = 512
NST = S // ST
NOC = HID // 128
SLOTW = 576
BASE_COUPLING = 0.1

ENGINES = ("pe", "dve", "act", "pool", "sp")


class Buf:
    __slots__ = ("name", "writer", "readers")

    def __init__(self, name=""):
        self.name = name
        self.writer = None
        self.readers = []


class Prog:

    def __init__(self, nc, same_engine_sync=()):
        self.nc = nc
        self.same_engine_sync = set(same_engine_sync)
        self.queues = {e: [] for e in ENGINES}
        self.scounts = {e: 0 for e in ENGINES}
        self.waited = {e: {} for e in ENGINES}
        self.sems = {}

    def _emit_waits(self, engine, deps):
        waits = []
        for skey, cnt in deps.items():
            if cnt <= 0:
                continue
            if skey == engine and engine not in self.same_engine_sync:
                continue
            if self.waited[engine].get(skey, 0) >= cnt:
                continue
            self.waited[engine][skey] = cnt
            waits.append((skey, cnt))
        return waits

    @staticmethod
    def _add_dep(deps, skey, c):
        if deps.get(skey, -1) < c:
            deps[skey] = c

    def op(self, engine, reads=(), writes=(), emit=None, inc=1):
        is_dma = inc == 16
        if is_dma:
            assert writes, "DMA op must declare a written buf"
            skey = "d_" + writes[0].name
        else:
            skey = engine
        deps = {}
        for b in reads:
            if b.writer is not None:
                self._add_dep(deps, *b.writer)
        for b in writes:
            if b.writer is not None:
                self._add_dep(deps, *b.writer)
            for rd in b.readers:
                self._add_dep(deps, *rd)
        waits = self._emit_waits(engine, deps)
        newcount = self.scounts.get(skey, 0) + inc
        self.scounts[skey] = newcount

        def thunk(eng, _w=waits, _e=emit, _i=inc, _sk=skey):
            for (sk, c) in _w:
                eng.wait_ge(self.sems[sk], c)
            ins = _e(eng)
            ins.then_inc(self.sems[_sk], _i)

        self.queues[engine].append(thunk)
        me = (skey, newcount)
        for b in reads:
            b.readers.append(me)
        for b in writes:
            b.writer = me
            b.readers = []
        return newcount

    def dma(self, engine, reads=(), writes=(), emit=None):
        return self.op(engine, reads, writes, emit, inc=16)

    def raw(self, engine, emit):
        self.queues[engine].append(emit)

    def barrier(self):
        snap = dict(self.scounts)
        for e in ENGINES:
            deps = {k: v for k, v in snap.items() if k != e}
            waits = self._emit_waits(e, deps)
            if not waits:
                continue

            def thunk(eng, _w=waits):
                for (sk, c) in _w:
                    eng.wait_ge(self.sems[sk], c)

            self.queues[e].append(thunk)

    def finalize(self):
        nc = self.nc
        stack = contextlib.ExitStack()
        for sk in self.scounts:
            self.sems[sk] = stack.enter_context(nc.semaphore(name=f"s_{sk}"))
        name_map = {"pe": "tensor", "dve": "vector", "act": "scalar",
                    "pool": "gpsimd", "sp": "sync"}
        with nc.Block() as block:
            for e in ENGINES:
                q = self.queues[e]

                def body(eng, _q=q):
                    for f in _q:
                        f(eng)

                getattr(block, name_map[e])(body)
        stack.close()
        return nc


import os


def build_core(phases=4, ablate=99, sub=99, subb=99):
    _acq = int(os.environ.get('ACQ', '4'))
    _natt = int(os.environ.get('NATT', '99'))
    nc = bass.Bass("TRN2", target_bir_lowering=False, debug=False)
    p = Prog(nc, same_engine_sync=tuple(
        x for x in os.environ.get('SAFE_ENGINES', 'act,dve,pool').split(',') if x))
    bufs = {}

    def B(name):
        if name not in bufs:
            bufs[name] = Buf(name)
        return bufs[name]

    def din(name, shape):
        return nc.dram_tensor(name, shape, F32, kind="ExternalInput").ap()

    x_in = din("x_fm", [HID, S])
    wqT = din("wqT", [128, 8, HID])
    wkT = din("wkT", [128, 8, HID])
    wvT = din("wvT", [128, 8, HID])
    wrelT = din("wrelT", [128, 8, HID])
    wpostT = din("wpostT", [128, 8, HID])
    posT = din("posT", [128, 8, CTX])
    pds2 = din("pds2", [128, 1])
    cn_w_d = [din("cn_w1", [128, 8, 1024]), din("cn_w2", [128, 8, 512]),
              din("cn_w3", [128, 4, 256]), din("cn_w4", [128, 2, 1])]
    cn_b_d = [din("cn_b1", [128, 8]), din("cn_b2", [128, 4]),
              din("cn_b3", [128, 2]), din("cn_b4", [1, 1])]
    gc_w_d = [din("gc_w1", [128, 8, 512]), din("gc_w2", [128, 4, 256]),
              din("gc_w3", [128, 2, 1])]
    gc_b_d = [din("gc_b1", [128, 4]), din("gc_b2", [128, 2]), din("gc_b3", [1, 1])]
    ar_in = din("ar", [1, 1])
    winT = din("winT", [128, 8, 3072])
    woutT = din("woutT", [128, 12, HID])
    alog_in = din("alog", [128, NCC, D_STATE])
    bmat_in = din("bmat", [128, NCC, D_STATE])
    cmat_in = din("cmat", [128, NCC, D_STATE])
    lnio_d = {"ig": din("ln_in_g", [128, 8]), "ib": din("ln_in_b", [128, 8]),
              "og": din("ln_out_g", [128, 8]), "ob": din("ln_out_b", [128, 8])}

    y_out = nc.dram_tensor("y_fm", [HID, S], F32, kind="ExternalOutput").ap()
    dbg_out = nc.dram_tensor("dbg", [4, 128, 1136], F32, kind="ExternalOutput").ap()
    _dbg = int(os.environ.get("DBG", "0"))

    xA = nc.dram_tensor("xA", [HID, S], F32, kind="Internal").ap()
    xB = nc.dram_tensor("xB", [HID, S], F32, kind="Internal").ap()
    xC = nc.dram_tensor("xC", [HID, S], F32, kind="Internal").ap()
    brC = nc.dram_tensor("brC", [HID, S], F32, kind="Internal").ap()
    bdsc = nc.dram_tensor("bdsc", [4, H, 2, 2304], F32, kind="Internal").ap()
    gt_d = nc.dram_tensor("gt_d", [D_INNER, S], BF16, kind="Internal").ap()
    tfh_d = nc.dram_tensor("tfh_d", [NST, ST], F32, kind="Internal").ap()
    attb_d = nc.dram_tensor("attb_d", [71, 2048], BF16, kind="Internal").ap()
    statsD = nc.dram_tensor("statsD", [6, 4, S], F32, kind="Internal").ap()

    IDENT = nc.alloc_sbuf_tensor("IDENT", [128, 128], BF16)
    ONESB = nc.alloc_sbuf_tensor("ONESB", [128, 1], BF16)
    ONESF = nc.alloc_sbuf_tensor("ONESF", [1, 128], F32)
    NONESF = nc.alloc_sbuf_tensor("NONESF", [1, 128], F32)

    LNROW = nc.alloc_sbuf_tensor("LNROW", [1, 2, ST], F32)
    STEV = nc.alloc_sbuf_tensor("STEV", [1, 2, ST], F32)
    GX = nc.alloc_sbuf_tensor("GX", [128, 2, 512], F32)
    GR = nc.alloc_sbuf_tensor("GR", [128, 2, 512], F32)
    GO = nc.alloc_sbuf_tensor("GO", [128, 2, 512], F32)
    GB = nc.alloc_sbuf_tensor("GB", [128, 2, 512], BF16)
    GQ = nc.alloc_sbuf_tensor("GQ", [128, 2, 512], BF16)
    PS = nc.alloc_psum_tensor("PS", [128, 4096], F32)

    LN1, LN2, LNI, LNO, LN4 = 0, 1, 2, 3, 4
    PSB = [B(f"psb{i}") for i in range(8)]

    pe_state = {"base": -1, "partial": False}

    def sep_mm(e):
        e.matmul(PS[96:97, 7 * 512:7 * 512 + 1], ONESB[:, :], ONESB[:, :],
                 start=True, stop=True, tile_position=(0, 96),
                 skip_group_check=True)

    def pe_guard(first_base, first_k, last_base, last_k, emit_fn):
        need = (pe_state["partial"] and first_k < 128
                and pe_state["base"] != first_base)
        pe_state["base"] = last_base
        pe_state["partial"] = last_k < 128
        if not need:
            return emit_fn

        def wrapped(e, _f=emit_fn):
            sep_mm(e)
            return _f(e)
        return wrapped

    def bank(i, w=1):
        return PS[:, i * 512:(i + w) * 512]

    def emit_stats_chunk(xbf_ap, xsq_ap, rd_bufs, ln, st, oc, last=7):
        def mkst(e):
            e.matmul(PS[0:1, 4 * 512:4 * 512 + 512], ONESB[:, :], xbf_ap,
                     start=(oc == 0), stop=(oc == last), skip_group_check=True)
            return e.matmul(PS[0:1, 5 * 512:5 * 512 + 512], ONESB[:, :], xsq_ap,
                            start=(oc == 0), stop=(oc == last),
                            skip_group_check=True)
        p.op("pe", reads=rd_bufs + [B("ONESB")], writes=[PSB[4], PSB[5]],
             emit=pe_guard(0, 128, 0, 128, mkst))
        if oc == last:
            p.op("dve", reads=[PSB[4]], writes=[B("STEV")],
                 emit=lambda e: e.tensor_copy(STEV[:, 0, :],
                                              PS[0:1, 4 * 512:4 * 512 + 512]))
            p.op("dve", reads=[PSB[5], B("STEV")], writes=[B("STEV")],
                 emit=lambda e: e.tensor_copy(STEV[:, 1, :],
                                              PS[0:1, 5 * 512:5 * 512 + 512]))
            p.dma("sp", reads=[B("STEV")], writes=[B("statsD")],
                  emit=lambda e, ln=ln, st=st: e.dma_start(
                      out=statsD[ln, 0:2, st * ST:(st + 1) * ST].unsqueeze(0),
                      in_=STEV[:, :, :]))

    def finalize_ln(ln):
        with nc.sbuf_tensor("FIN", [1, 2, S], F32) as FIN:
            bf_ = B(f"FIN{ln}")
            p.dma("sp", reads=[B("statsD")], writes=[bf_],
                  emit=lambda e: e.dma_start(out=FIN[:, :, :].squeeze(0),
                                             in_=statsD[ln, 0:2, :]))
            p.op("dve", reads=[bf_], writes=[bf_],
                 emit=lambda e: e.tensor_single_scalar(FIN[:, 0, :], FIN[:, 0, :],
                                                       1.0 / HID, ALU.mult))
            p.op("dve", reads=[bf_], writes=[bf_],
                 emit=lambda e: e.tensor_single_scalar(FIN[:, 1, :], FIN[:, 1, :],
                                                       1.0 / HID, ALU.mult))
            p.op("dve", reads=[bf_], writes=[bf_],
                 emit=lambda e: e.scalar_tensor_tensor(FIN[:, 1, :], FIN[:, 0, :],
                                                       1.0, FIN[:, 1, :],
                                                       ALU.mult, ALU.subtract))
            raise AssertionError("placeholder")

    def finalize_ln(ln):
        with nc.sbuf_tensor(f"FIN{ln}", [1, 3, S], F32) as FIN:
            bf_ = B(f"FIN{ln}")
            p.dma("sp", reads=[B("statsD")], writes=[bf_],
                  emit=lambda e: e.dma_start(out=FIN[:, 0:2, :],
                                             in_=statsD[ln, 0:2, :].unsqueeze(0)))
            p.op("dve", reads=[bf_], writes=[bf_],
                 emit=lambda e: e.tensor_single_scalar(FIN[:, 0, :], FIN[:, 0, :],
                                                       1.0 / HID, ALU.mult))
            p.op("dve", reads=[bf_], writes=[bf_],
                 emit=lambda e: e.tensor_tensor(FIN[:, 2, :], FIN[:, 0, :],
                                                FIN[:, 0, :], ALU.mult))
            p.op("dve", reads=[bf_], writes=[bf_],
                 emit=lambda e: e.scalar_tensor_tensor(FIN[:, 2, :], FIN[:, 1, :],
                                                       1.0 / HID, FIN[:, 2, :],
                                                       ALU.mult, ALU.subtract))
            p.op("dve", reads=[bf_], writes=[bf_],
                 emit=lambda e: e.tensor_single_scalar(FIN[:, 2, :], FIN[:, 2, :],
                                                       LN_EPS, ALU.add))
            p.op("act", reads=[bf_], writes=[bf_],
                 emit=lambda e: e.activation(FIN[:, 2, :], FIN[:, 2, :], AF.Sqrt))
            p.op("dve", reads=[bf_], writes=[bf_],
                 emit=lambda e: e.reciprocal(FIN[:, 2, :], FIN[:, 2, :]))
            p.op("dve", reads=[bf_], writes=[bf_],
                 emit=lambda e: e.tensor_tensor(FIN[:, 0, :], FIN[:, 0, :],
                                                FIN[:, 2, :], ALU.mult))
            p.dma("sp", reads=[bf_], writes=[B("statsD")],
                  emit=lambda e: e.dma_start(
                      out=statsD[ln, 2, :].unsqueeze(0), in_=FIN[:, 0, :]))
            p.dma("sp", reads=[bf_], writes=[B("statsD")],
                  emit=lambda e: e.dma_start(
                      out=statsD[ln, 3, :].unsqueeze(0), in_=FIN[:, 2, :]))
        p.barrier()

    def emit_ln_bcast(ln, st, bank_r=2, bank_m=3):
        p.dma("sp", reads=[B("statsD")], writes=[B("LNROW")],
              emit=lambda e, ln=ln, st=st: e.dma_start(
                  out=LNROW[:, :, :],
                  in_=statsD[ln, 2:4, st * ST:(st + 1) * ST].unsqueeze(0)))
        p.op("pe", reads=[B("LNROW"), B("ONESF")], writes=[PSB[bank_r]],
             emit=pe_guard(0, 1, 0, 1,
                           lambda e: e.matmul(bank(bank_r), ONESF[:, :],
                                              LNROW[0:1, 1, :],
                                              start=True, stop=True)))
        p.op("pe", reads=[B("LNROW"), B("ONESF")], writes=[PSB[bank_m]],
             emit=pe_guard(0, 1, 0, 1,
                           lambda e: e.matmul(bank(bank_m), ONESF[:, :],
                                              LNROW[0:1, 0, :],
                                              start=True, stop=True)))

    def proj_gemm(dst_emit, lhsW, lhsbuf, rhsT, rhsbuf, n_oc, k_ic=8,
                  banks=(0, 1), n=512):
        for oc in range(n_oc):
            bk = banks[oc % 2]
            def mk(oc=oc, bk=bk):
                def em(e):
                    ins = None
                    for ic in range(k_ic):
                        ins = e.matmul(PS[:, bk * 512:bk * 512 + n],
                                       lhsW[:, ic, oc * 128:(oc + 1) * 128],
                                       rhsT(ic), start=(ic == 0),
                                       stop=(ic == k_ic - 1))
                    return ins
                return em
            p.op("pe", reads=[lhsbuf, rhsbuf], writes=[PSB[bk]],
                 emit=pe_guard(0, 128, 0, 128, mk()))
            dst_emit(oc, bk)

    p.op("dve", writes=[B("ZEROT")], emit=lambda e: e.memset(ZEROT[:, :], 0.0))
    p.op("dve", writes=[B("ONESF")], emit=lambda e: e.memset(ONESF[:, :], 1.0))
    p.op("dve", writes=[B("NONESF")], emit=lambda e: e.memset(NONESF[:, :], -1.0))
    p.op("pool", writes=[B("ONESB")], emit=lambda e: e.memset(ONESB[:, :], 1.0))

    scopeA = contextlib.ExitStack()

    def sa(nm, shape, dt):
        return scopeA.enter_context(nc.sbuf_tensor(nm, shape, dt))

    WQ = sa("WQ", [128, 8, HID], BF16)
    WK = sa("WK", [128, 8, HID], BF16)
    WV = sa("WV", [128, 8, HID], BF16)
    WPOST = sa("WPOST", [128, 8, HID], BF16)
    RELK = sa("RELK", [128, 8, CTX], BF16)
    QSC = sa("QSC", [128, 1], F32)

    scope0 = contextlib.ExitStack()
    IOTP = scope0.enter_context(nc.sbuf_tensor("IOTP", [128, 128], F32))
    IOTF = scope0.enter_context(nc.sbuf_tensor("IOTF", [128, 128], F32))
    POSB = scope0.enter_context(nc.sbuf_tensor("POSB", [128, 8, CTX], BF16))
    WREL = scope0.enter_context(nc.sbuf_tensor("WREL", [128, 8, HID], BF16))
    PDS = scope0.enter_context(nc.sbuf_tensor("PDS", [128, 4], F32))
    ZEROT = scope0.enter_context(nc.sbuf_tensor("ZEROT", [128, 512], F32))

    p.op("pool", writes=[B("IOTP")],
         emit=lambda e: e.iota(IOTP[:, :], pattern=[[0, 128]], channel_multiplier=1,
                               allow_small_or_imprecise_dtypes=True))
    p.op("pool", writes=[B("IOTF")],
         emit=lambda e: e.iota(IOTF[:, :], pattern=[[1, 128]], channel_multiplier=0,
                               allow_small_or_imprecise_dtypes=True))
    p.op("dve", reads=[B("IOTP"), B("IOTF")], writes=[B("IDENT")],
         emit=lambda e: e.tensor_tensor(IDENT[:, :], IOTP[:, :], IOTF[:, :],
                                        ALU.is_equal))

    for nm, dst, src in (("WQ", WQ, wqT), ("WK", WK, wkT), ("WV", WV, wvT),
                         ("WPOST", WPOST, wpostT)):
        p.dma("pool", writes=[B(nm)],
              emit=lambda e, d=dst, s=src: e.dma_start(out=d[:, :, :], in_=s))
    p.dma("pool", writes=[B("POSB")],
          emit=lambda e: e.dma_start(out=POSB[:, :, :], in_=posT))
    p.dma("pool", writes=[B("WREL")],
          emit=lambda e: e.dma_start(out=WREL[:, :, :], in_=wrelT))
    for oc in range(8):
        def mk(oc=oc):
            def em(e):
                ins = None
                for ic in range(8):
                    ins = e.matmul(PS[:, 6 * 512:6 * 512 + CTX],
                                   WREL[:, ic, oc * 128:(oc + 1) * 128],
                                   POSB[:, ic, :], start=(ic == 0), stop=(ic == 7))
                return ins
            return em
        p.op("pe", reads=[B("WREL"), B("POSB")], writes=[PSB[6]],
             emit=pe_guard(0, 128, 0, 128, mk()))
        p.op("dve", reads=[PSB[6]], writes=[B("RELK")],
             emit=lambda e, oc=oc: e.tensor_copy(RELK[:, oc, :],
                                                 PS[:, 6 * 512:6 * 512 + CTX]))

    p.dma("sp", writes=[B("PDS")],
          emit=lambda e: e.dma_start(out=PDS[:, 0:1], in_=pds2))
    p.op("act", reads=[B("PDS")], writes=[B("PDSa")],
         emit=lambda e: e.activation(PDS[:, 1:2], PDS[:, 0:1], AF.Abs))
    p.op("act", reads=[B("PDSa")], writes=[B("PDSe")],
         emit=lambda e: e.activation(PDS[:, 2:3], PDS[:, 1:2], AF.Exp, scale=-1.0))
    p.op("act", reads=[B("PDSe")], writes=[B("PDSl")],
         emit=lambda e: e.activation(PDS[:, 3:4], PDS[:, 2:3], AF.Ln, bias=1.0))
    p.op("dve", reads=[B("PDS")], writes=[B("PDSr")],
         emit=lambda e: e.tensor_single_scalar(PDS[:, 0:1], PDS[:, 0:1], 0.0,
                                               ALU.max))
    p.op("dve", reads=[B("PDSr"), B("PDSl")], writes=[B("QSC")],
         emit=lambda e: e.tensor_tensor(QSC[:, :], PDS[:, 0:1], PDS[:, 3:4],
                                        ALU.add))
    p.op("dve", reads=[B("QSC")], writes=[B("QSC")],
         emit=lambda e: e.tensor_single_scalar(QSC[:, :], QSC[:, :], Q_SCALE,
                                               ALU.mult))

    TOTBD = 4 * 2 * H * 2304
    nfull = TOTBD // 65536
    for i in range(nfull + 1):
        n0 = i * 65536
        n1 = min((i + 1) * 65536, TOTBD)
        if n1 <= n0:
            continue
        rows = (n1 - n0) // 512
        p.dma("sp", reads=[B("ZEROT")],
              writes=[B(f"bdz{i}")] + [B(f"bdsc{q}") for q in range(4)],
              emit=lambda e, n0=n0, n1=n1, rows=rows: e.dma_start(
                  out=bdsc.rearrange("a b c d -> (a b c d)")[n0:n1]
                  .rearrange("(r c) -> r c", c=512),
                  in_=ZEROT[0:rows, 0:512]))

    for st in range(NST):
        for oc in range(NOC):
            pg = oc % 2
            bx, bb, bq = B(f"GX{pg}"), B(f"GB{pg}"), B(f"GQ{pg}")
            p.dma("sp", writes=[bx],
                  emit=lambda e, st=st, oc=oc, pg=pg: e.dma_start(
                      out=GX[:, pg, :],
                      in_=x_in[oc * 128:(oc + 1) * 128, st * ST:(st + 1) * ST]))
            p.op("act", reads=[bx], writes=[bb],
                 emit=lambda e, pg=pg: e.activation(GB[:, pg, :], GX[:, pg, :],
                                                    AF.Copy))
            p.op("act", reads=[bb], writes=[bq],
                 emit=lambda e, pg=pg: e.activation(GQ[:, pg, :], GB[:, pg, :],
                                                    AF.Square))
            emit_stats_chunk(GB[:, pg, :], GQ[:, pg, :], [bb, bq], LN1, st, oc)
    finalize_ln(LN1)
    scope0.close()

    XLN = sa("XLN", [128, 8, 512], BF16)
    QR = sa("QR", [128, 2, 8, 512], BF16)
    KSL = sa("KSL", [128, 2, 8, SLOTW], BF16)
    VSL = sa("VSL", [128, 2, 8, SLOTW], BF16)
    V2I = sa("V2I", [128, H, 512], BF16)
    BDS = sa("BDS", [128, H, 2, CTX], F32)
    SSB = sa("SSB", [128, 2, 16 * CTX], F32)
    ATB = sa("ATB", [128, 2, 16 * CTX], BF16)
    ATT = sa("ATT", [71, 2, 1024], BF16)
    ATTB = sa("ATTB", [128, 2, 1024], BF16)
    AOUT = sa("AOUT", [128, 8, 512], BF16)
    RSUM = sa("RSUM", [128, 2, 16], F32)

    def kv_overlap(i, W, nm):
        r = i % 2
        rp = (i - 1) % 2
        if i > 0:
            p.dma("sp", reads=[B(f"{nm}{r}")], writes=[B(f"{nm}{rp}")],
                  emit=lambda e: e.dma_start(out=W[:, rp, :, 543:576],
                                             in_=W[:, r, :, 31:64]))
            p.dma("sp", reads=[B(f"{nm}{rp}")], writes=[B(f"{nm}{r}")],
                  emit=lambda e: e.dma_start(out=W[:, r, :, 0:31],
                                             in_=W[:, rp, :, 512:543]))
        else:
            p.op("pool", writes=[B(f"{nm}{r}")],
                 emit=lambda e: e.memset(W[:, r, :, 0:31], 0.0))
        if i == NST - 1:
            p.op("pool", reads=[B(f"{nm}{r}")], writes=[B(f"{nm}{r}")],
                 emit=lambda e: e.memset(W[:, r, :, 543:576], 0.0))

    for i in range(NST + 1):
        if i < NST:
            st = i
            r = st % 2
            xp = st % 2
            emit_ln_bcast(LN1, st, 2, 3)
            for oc in range(NOC):
                cp = oc % 2
                bx = B(f"GX{cp}")
                p.dma("sp", writes=[bx],
                      emit=lambda e, st=st, oc=oc, cp=cp: e.dma_start(
                          out=GX[:, cp, :],
                          in_=x_in[oc * 128:(oc + 1) * 128, st * ST:(st + 1) * ST]))
                p.op("dve", reads=[bx, PSB[2]], writes=[bx],
                     emit=lambda e, cp=cp: e.tensor_tensor(
                         GX[:, cp, :], GX[:, cp, :], bank(2), ALU.mult))
                p.op("dve", reads=[bx, PSB[3]], writes=[B("XLN")],
                     emit=lambda e, oc=oc, cp=cp: e.tensor_tensor(
                         XLN[:, oc, :], GX[:, cp, :], bank(3), ALU.subtract))

            def qdst(oc, bk, xp=xp):
                p.op("act", reads=[PSB[bk], B("QSC")], writes=[B(f"QR{xp}")],
                     emit=lambda e, oc=oc, bk=bk: e.activation(
                         QR[:, xp, oc, :], bank(bk), AF.Copy, scale=QSC[:, 0:1]))
            proj_gemm(qdst, WQ, B("WQ"), lambda ic: XLN[:, ic, :], B("XLN"), 8)

            def kdst(oc, bk, r=r):
                p.op("act", reads=[PSB[bk]], writes=[B(f"K{r}")],
                     emit=lambda e, oc=oc, bk=bk: e.activation(
                         KSL[:, r, oc, 31:543], bank(bk), AF.Copy, scale=K_SCALE))
            proj_gemm(kdst, WK, B("WK"), lambda ic: XLN[:, ic, :], B("XLN"), 8)

            def vdst(oc, bk, r=r):
                p.op("act", reads=[PSB[bk]], writes=[B(f"V{r}")],
                     emit=lambda e, oc=oc, bk=bk: e.activation(
                         VSL[:, r, oc, 31:543], bank(bk), AF.Copy))
            proj_gemm(vdst, WV, B("WV"), lambda ic: XLN[:, ic, :], B("XLN"), 8)
            kv_overlap(st, KSL, "K")
            kv_overlap(st, VSL, "V")

        if i < NST and ablate < 1:
            continue
        if i >= 1 and ablate >= 2 and i <= _natt:
            j = i - 1
            rj = j % 2
            xpj = j % 2
            _vord = [h for h in range(H) if h % 2 == 0] + \
                    [h for h in range(H) if h % 2 == 1]
            for h in (_vord if ablate >= 3 else []):
                hr = (h % 2) * 64
                hc = h // 2
                def mkv2(hr=hr, hc=hc, rj=rj):
                    def em(e):
                        ins = None
                        for m in range(8):
                            ins = e.transpose(
                                bank(6).bitcast(BF16)[:, m * 64:(m + 1) * 64],
                                VSL[hr:hr + 64, rj, hc, m * 64:m * 64 + 128],
                                IDENT[hr:hr + 64, hr:hr + 64])
                        return ins
                    return em
                p.op("pe", reads=[B(f"V{rj}"), B("IDENT")], writes=[PSB[6]],
                     emit=pe_guard(hr, 64, hr, 64, mkv2()))
                p.op("act", reads=[PSB[6]], writes=[B(f"V2I{h}")],
                     emit=lambda e, h=h: e.activation(
                         V2I[:, h, :], bank(6).bitcast(BF16)[:, 0:512], AF.Copy))
            for gh in range(2 if ablate >= 4 else 0):
                for h in ([x for x in range(H) if x % 2 == 0] +
                          [x for x in range(H) if x % 2 == 1]):
                    hr = (h % 2) * 64
                    hc = h // 2
                    def mkbd(h=h, hr=hr, hc=hc, xpj=xpj, gh=gh):
                        def em(e):
                            ins = None
                            for gl in range(2):
                                g = gh * 2 + gl
                                ins = e.matmul(
                                    PS[:, 6 * 512 + gl * 128:
                                       6 * 512 + gl * 128 + CTX],
                                    QR[hr:hr + 64, xpj, hc,
                                       g * 128:(g + 1) * 128],
                                    RELK[hr:hr + 64, hc, :],
                                    start=True, stop=True)
                            return ins
                        return em
                    p.op("pe", reads=[B(f"QR{xpj}"), B("RELK")], writes=[PSB[6]],
                         emit=pe_guard(hr, 64, hr, 64, mkbd()))
                    p.op("act", reads=[PSB[6]], writes=[B("BDS")],
                         emit=lambda e, h=h: e.activation(
                             BDS[:, h, :, :],
                             bank(6).rearrange("p (g c) -> p g c",
                                               g=4)[:, 0:2, 0:CTX], AF.Copy))
                for q in range(4 if sub >= 1 else 0):
                    p.dma("sp", reads=[B("BDS")], writes=[B(f"bdsc{q}")],
                          emit=lambda e, q=q: e.dma_start(
                              out=bdsc[q, :, :, :].rearrange(
                                  "h g (c l) -> c h g l", l=72)[:, :, :, 0:71],
                              in_=BDS[q * 32:(q + 1) * 32, :, :, :]))
                for q in range(4 if sub >= 1 else 0):
                    p.dma("sp", reads=[B(f"bdsc{q}")], writes=[B("BDS")],
                          emit=lambda e, q=q: e.dma_start(
                              out=BDS[q * 32:(q + 1) * 32, :, :, :],
                              in_=bdsc[q, :, :, 0:2272].rearrange(
                                  "h g (c l) -> c h g l", l=71)))
                for gl in range(2 if sub >= 2 else 0):
                    g = gh * 2 + gl
                    for half in range(2):
                        sb = 2 if half == 0 else 4
                        h0 = half * 8
                        for hh in ([x for x in range(8) if (h0 + x) % 2 == 0] +
                                   [x for x in range(8) if (h0 + x) % 2 == 1]):
                            h = h0 + hh
                            hr = (h % 2) * 64
                            hc = h // 2
                            def mkac(g=g, hr=hr, hc=hc, hh=hh, sb=sb,
                                     xpj=xpj, rj=rj):
                                def em(e):
                                    ins = None
                                    for q in range(_acq):
                                        tp = ({"tile_position": (hr, 96)}
                                              if q == 3 else {})
                                        win = g * 128 + q * 32
                                        ins = e.matmul(
                                            PS[q * 32:(q + 1) * 32,
                                               sb * 512 + hh * 128:
                                               sb * 512 + hh * 128 + CTX],
                                            QR[hr:hr + 64, xpj, hc,
                                               g * 128 + q * 32:
                                               g * 128 + q * 32 + 32],
                                            KSL[hr:hr + 64, rj, hc,
                                                win:win + CTX],
                                            start=True, stop=True, **tp)
                                    return ins
                                return em
                            p.op("pe", reads=[B(f"QR{xpj}"), B(f"K{rj}")],
                                 writes=[PSB[sb], PSB[sb + 1]],
                                 emit=pe_guard(hr, 64, hr, 64, mkac()))
                        if subb < 1:
                            continue
                        gp = g % 2
                        p.op("dve", reads=[PSB[sb], PSB[sb + 1], B("BDS")],
                             writes=[B(f"SSB{gp}")],
                             emit=lambda e, sb=sb, h0=h0, gl=gl, gp=gp: e.tensor_tensor(
                                 SSB[:, gp, h0 * CTX:(h0 + 8) * CTX].rearrange(
                                     "p (h c) -> p h c", h=8),
                                 bank(sb, 2).rearrange(
                                     "p (h c) -> p h c", h=8)[:, :, 0:CTX],
                                 BDS[:, h0:h0 + 8, gl, :], ALU.add))
                        if _dbg and j == 0 and g == 0 and half == 1:
                            p.dma("sp", reads=[B("SSB")], writes=[B("dbg0")],
                                  emit=lambda e: e.dma_start(
                                      out=dbg_out[0], in_=SSB[:, :]))
                        if subb >= 2:
                            p.op("act", reads=[B(f"SSB{gp}")], writes=[B(f"SSB{gp}")],
                                 emit=lambda e, h0=h0, gp=gp: e.activation(
                                     SSB[:, gp, h0 * CTX:(h0 + 8) * CTX],
                                     SSB[:, gp, h0 * CTX:(h0 + 8) * CTX],
                                     AF.Tanh, scale=1.0 / SOFTCAP))
                        if subb >= 2:
                            p.op("act", reads=[B(f"SSB{gp}")], writes=[B(f"SSB{gp}")],
                                 emit=lambda e, h0=h0, gp=gp: e.activation(
                                     SSB[:, gp, h0 * CTX:(h0 + 8) * CTX],
                                     SSB[:, gp, h0 * CTX:(h0 + 8) * CTX],
                                     AF.Exp, scale=SOFTCAP))
                        if subb < 3:
                            continue
                        p.op("dve", reads=[B(f"SSB{gp}")], writes=[B(f"RSUM{gp}")],
                             emit=lambda e, h0=h0, gp=gp: e.reduce_sum(
                                 RSUM[:, gp, h0:h0 + 8].unsqueeze(2),
                                 SSB[:, gp, h0 * CTX:(h0 + 8) * CTX].rearrange(
                                     "p (h c) -> p h c", h=8), axis=AX.X))
                    if _dbg and j == 0 and g == 0:
                        p.dma("sp", reads=[B("SSB")], writes=[B("dbg1")],
                              emit=lambda e: e.dma_start(
                                  out=dbg_out[1], in_=SSB[:, :]))
                        p.dma("sp", reads=[B("RSUM")], writes=[B("dbg2")],
                              emit=lambda e: e.dma_start(
                                  out=dbg_out[2][:, 0:16], in_=RSUM[:, :]))
                        p.dma("sp", reads=[B("BDS")], writes=[B("dbg3")],
                              emit=lambda e: e.dma_start(
                                  out=dbg_out[3][:, 0:16 * CTX]
                                  .rearrange("p (h c) -> p h c", h=16),
                                  in_=BDS[:, :, 0, :]))
                    if subb < 4:
                        continue
                    gp = g % 2
                    p.op("dve", reads=[B(f"RSUM{gp}")], writes=[B(f"RSUM{gp}")],
                         emit=lambda e, gp=gp: e.reciprocal(RSUM[:, gp, :],
                                                            RSUM[:, gp, :]))
                    def mkscale(gp=gp):
                        def em(e):
                            ins = None
                            for h in range(H):
                                ins = e.tensor_scalar(
                                    ATB[:, gp, h * CTX:(h + 1) * CTX],
                                    SSB[:, gp, h * CTX:(h + 1) * CTX],
                                    RSUM[:, gp, h:h + 1], None, ALU.mult)
                            return ins
                        return em
                    p.op("dve", reads=[B(f"SSB{gp}"), B(f"RSUM{gp}")],
                         writes=[B(f"ATB{gp}")], emit=mkscale())
                    for half in range(2 if sub >= 3 else 0):
                        def mktr(half=half, gp=gp):
                            def em(e):
                                ins = None
                                for hh in range(8):
                                    h = half * 8 + hh
                                    ins = e.transpose(
                                        bank(7).bitcast(BF16)[0:71,
                                            hh * 128:hh * 128 + 128],
                                        ATB[:, gp, h * CTX:(h + 1) * CTX],
                                        IDENT[:, :])
                                return ins
                            return em
                        p.op("pe", reads=[B(f"ATB{gp}"), B("IDENT")], writes=[PSB[7]],
                             emit=pe_guard(0, 128, 0, 128, mktr()))
                        p.op("act", reads=[PSB[7]], writes=[B(f"ATT{half}")],
                             emit=lambda e, half=half, gp=gp: e.activation(
                                 ATT[:, half, :],
                                 bank(7).bitcast(BF16)[0:71, 0:1024], AF.Copy))
                        def mktrs(half=half, gp=gp):
                            def em(e):
                                ins = None
                                for hh in range(8):
                                    h = half * 8 + hh
                                    e.transpose(
                                        bank(6).bitcast(BF16)[32:64,
                                            hh * 128:hh * 128 + 128],
                                        ATB[:, gp, h * CTX:h * CTX + 32],
                                        IDENT[:, :])
                                    ins = e.transpose(
                                        bank(6).bitcast(BF16)[64:103,
                                            hh * 128:hh * 128 + 128],
                                        ATB[:, gp, h * CTX + 32:h * CTX + CTX],
                                        IDENT[:, :])
                                return ins
                            return em
                        p.op("pe", reads=[B(f"ATB{gp}"), B("IDENT")], writes=[PSB[6]],
                             emit=pe_guard(0, 128, 0, 128, mktrs()))
                        p.op("act", reads=[PSB[6]], writes=[B(f"ATTB{half}")],
                             emit=lambda e, half=half, gp=gp: e.activation(
                                 ATTB[32:64, half, :],
                                 bank(6).bitcast(BF16)[32:64, 0:1024], AF.Copy))
                        p.op("act", reads=[PSB[6]], writes=[B(f"ATTB{half}")],
                             emit=lambda e, half=half, gp=gp: e.activation(
                                 ATTB[64:103, half, :],
                                 bank(6).bitcast(BF16)[64:103, 0:1024], AF.Copy))
                    if sub < 3:
                        continue
                    if sub < 4:
                        continue
                    def mkav(g=g):
                        def em(e):
                            ins = None
                            for h in range(H):
                                half = h // 8
                                hh = h % 8
                                pr = h // 2
                                for q in (0, 2):
                                    sloc = g * 128 + q * 32
                                    m = sloc // 64
                                    ins = e.matmul(
                                        PS[(h % 2) * 64:(h % 2) * 64 + 64,
                                           2 * 512 + pr * 128 + q * 32:
                                           2 * 512 + pr * 128 + q * 32 + 32],
                                        V2I[0:CTX, h, m * 64:(m + 1) * 64],
                                        ATT[0:71, half,
                                            hh * 128 + q * 32:hh * 128 + q * 32 + 32],
                                        start=True, stop=True)
                            sep_mm(e)
                            for h in range(H):
                                half = h // 8
                                hh = h % 8
                                pr = h // 2
                                for q in (1, 3):
                                    sloc = g * 128 + q * 32
                                    m = sloc // 64
                                    ins = e.matmul(
                                        PS[(h % 2) * 64:(h % 2) * 64 + 64,
                                           2 * 512 + pr * 128 + q * 32:
                                           2 * 512 + pr * 128 + q * 32 + 32],
                                        V2I[32:64, h, m * 64:(m + 1) * 64],
                                        ATTB[32:64, half,
                                             hh * 128 + q * 32:hh * 128 + q * 32 + 32],
                                        start=True, stop=True,
                                        skip_group_check=True)
                            sep_mm(e)
                            for h in range(H):
                                half = h // 8
                                hh = h % 8
                                pr = h // 2
                                for q in (1, 3):
                                    sloc = g * 128 + q * 32
                                    m = sloc // 64
                                    ins = e.matmul(
                                        PS[(h % 2) * 64:(h % 2) * 64 + 64,
                                           4 * 512 + pr * 64 + (q // 2) * 32:
                                           4 * 512 + pr * 64 + (q // 2) * 32 + 32],
                                        V2I[64:103, h, m * 64:(m + 1) * 64],
                                        ATTB[64:103, half,
                                             hh * 128 + q * 32:hh * 128 + q * 32 + 32],
                                        start=True, stop=True,
                                        skip_group_check=True)
                            return ins
                        return em
                    p.op("pe",
                         reads=[B("ATT0"), B("ATT1"),
                                B("ATTB0"), B("ATTB1")] +
                               [B(f"V2I{h}") for h in range(H)],
                         writes=[PSB[2], PSB[3], PSB[4]],
                         emit=pe_guard(0, 71, 64, 39, mkav()))
                    p.op("dve", reads=[PSB[2], PSB[3]], writes=[B("AOUT")],
                         emit=lambda e, g=g: e.tensor_copy(
                             AOUT[:, :, g * 128:(g + 1) * 128],
                             bank(2, 2).rearrange("p (r t) -> p r t", r=8)))
                    p.op("dve", reads=[PSB[4], B("AOUT")], writes=[B("AOUT")],
                         emit=lambda e, g=g: e.tensor_tensor(
                             AOUT[:, :, g * 128:(g + 1) * 128].rearrange(
                                 "p r (q t) -> p r q t", q=4)[:, :, 1::2, :],
                             bank(4).rearrange("p (r q t) -> p r q t", r=8, q=2),
                             AOUT[:, :, g * 128:(g + 1) * 128].rearrange(
                                 "p r (q t) -> p r q t", q=4)[:, :, 1::2, :],
                             ALU.add))
            if ablate < 5:
                continue
            def postdst(oc, bk, j=j):
                cp = oc % 2
                bx = B(f"GR{cp}")
                p.dma("sp", writes=[bx],
                      emit=lambda e, j=j, oc=oc, cp=cp: e.dma_start(
                          out=GR[:, cp, :],
                          in_=x_in[oc * 128:(oc + 1) * 128, j * ST:(j + 1) * ST]))
                b1 = B(f"GO{cp}")
                p.op("dve", reads=[PSB[bk], bx], writes=[b1],
                     emit=lambda e, bk=bk, cp=cp: e.tensor_tensor(
                         GO[:, cp, :], bank(bk), GR[:, cp, :], ALU.add))
                p.dma("sp", reads=[b1], writes=[B("xA")],
                      emit=lambda e, j=j, oc=oc, cp=cp: e.dma_start(
                          out=xA[oc * 128:(oc + 1) * 128, j * ST:(j + 1) * ST],
                          in_=GO[:, cp, :]))
                bb = B(f"GB{cp}")
                bq = B(f"GQ{cp}")
                p.op("act", reads=[b1], writes=[bb],
                     emit=lambda e, cp=cp: e.activation(GB[:, cp, :],
                                                        GO[:, cp, :], AF.Copy))
                p.op("act", reads=[bb], writes=[bq],
                     emit=lambda e, cp=cp: e.activation(GQ[:, cp, :],
                                                        GB[:, cp, :], AF.Square))
                emit_stats_chunk(GB[:, cp, :], GQ[:, cp, :], [bb, bq], LN2, j, oc)
            proj_gemm(postdst, WPOST, B("WPOST"), lambda ic: AOUT[:, ic, :],
                      B("AOUT"), 8)

    p.barrier()
    scopeA.close()

    def ffn_phase(tag, x_src, x_dst, ln_src, ln_stats_out, dst_buf_name):
        SH = S // 2
        finalize_ln(ln_src)
        sc = contextlib.ExitStack()

        def sf(nm, shape, dt):
            return sc.enter_context(nc.sbuf_tensor(tag + nm, shape, dt))

        OUT = sf("OUT", [128, 8, SH], F32)
        OUTB = sf("OUTB", [128, 8, 512], BF16)
        HB1 = sf("HB1", [128, 8, 512], BF16)
        HB2 = sf("HB2", [128, 4, 512], BF16)
        HB3 = sf("HB3", [128, 2, 512], BF16)
        CW1 = sf("CW1", [128, 8, 1024], BF16)
        CW2 = sf("CW2", [128, 8, 512], BF16)
        CW3 = sf("CW3", [128, 4, 256], BF16)
        CW4 = sf("CW4", [128, 2, 1], BF16)
        CBX = sf("CBX", [128, 16], F32)
        GW1 = sf("GW1", [128, 8, 512], BF16)
        GW2 = sf("GW2", [128, 4, 256], BF16)
        GW3 = sf("GW3", [128, 2, 1], BF16)
        GBX = sf("GBX", [128, 8], F32)
        TFH = sf("TFH", [1, 1, 512], F32)
        CPL = sf("CPL", [1, 1, 2048], F32)
        GACC = sf("GACC", [128, 8, 8], F32)
        MEANT = sf("MEANT", [128, 8], F32)
        MEANB = sf("MEANB", [128, 8], BF16)
        GH = sf("GH", [128, 8], BF16)
        SML = sf("SML", [1, 8], F32)
        TMPU = sf("TMPU", [128, 2048], F32)
        TMPV = sf("TMPV", [128, 2048], F32)

        TB = lambda n: B(tag + n)

        for nm, dst, srcd in (("CW1", CW1, cn_w_d[0]), ("CW2", CW2, cn_w_d[1]),
                              ("CW3", CW3, cn_w_d[2]), ("CW4", CW4, cn_w_d[3]),
                              ("GW1", GW1, gc_w_d[0]), ("GW2", GW2, gc_w_d[1]),
                              ("GW3", GW3, gc_w_d[2])):
            p.dma("pool", writes=[TB(nm)],
                  emit=lambda e, d=dst, s=srcd: e.dma_start(out=d[:, :, :], in_=s))
        p.dma("sp", writes=[TB("CBX1")],
              emit=lambda e: e.dma_start(out=CBX[:, 0:8], in_=cn_b_d[0]))
        p.dma("sp", writes=[TB("CBX2")],
              emit=lambda e: e.dma_start(out=CBX[:, 8:12], in_=cn_b_d[1]))
        p.dma("sp", writes=[TB("CBX3")],
              emit=lambda e: e.dma_start(out=CBX[:, 12:14], in_=cn_b_d[2]))
        p.dma("sp", writes=[TB("SMLb4")],
              emit=lambda e: e.dma_start(out=SML[0:1, 0:1], in_=cn_b_d[3]))
        p.dma("sp", writes=[TB("GBX1")],
              emit=lambda e: e.dma_start(out=GBX[:, 0:4], in_=gc_b_d[0]))
        p.dma("sp", writes=[TB("GBX2")],
              emit=lambda e: e.dma_start(out=GBX[:, 4:6], in_=gc_b_d[1]))
        p.dma("sp", writes=[TB("SMLg3")],
              emit=lambda e: e.dma_start(out=SML[0:1, 1:2], in_=gc_b_d[2]))
        p.dma("sp", writes=[TB("SMLar")],
              emit=lambda e: e.dma_start(out=SML[0:1, 3:4], in_=ar_in))
        p.op("dve", reads=[TB("SMLb4")], writes=[TB("SMLb4")],
             emit=lambda e: e.tensor_single_scalar(SML[0:1, 0:1], SML[0:1, 0:1],
                                                   0.5, ALU.mult))
        p.op("dve", reads=[TB("SMLg3")], writes=[TB("SMLg3")],
             emit=lambda e: e.tensor_single_scalar(SML[0:1, 1:2], SML[0:1, 1:2],
                                                   0.5, ALU.mult))
        p.op("dve", reads=[TB("SMLar")], writes=[TB("SMLar3")],
             emit=lambda e: e.tensor_single_scalar(SML[0:1, 4:5], SML[0:1, 3:4],
                                                   0.3, ALU.mult))

        for st in range(NST):
            emit_ln_bcast(ln_src, st, 2, 3)
            for oc in range(NOC):
                cp = oc % 2
                bx = B(f"GX{cp}")
                bo = B(f"GO{cp}")
                bb = B(f"GB{cp}")
                p.dma("sp", writes=[bx],
                      emit=lambda e, st=st, oc=oc, cp=cp: e.dma_start(
                          out=GX[:, cp, :],
                          in_=x_src[oc * 128:(oc + 1) * 128,
                                    st * ST:(st + 1) * ST]))
                p.op("dve", reads=[bx, PSB[2]], writes=[bx],
                     emit=lambda e, cp=cp: e.tensor_tensor(
                         GX[:, cp, :], GX[:, cp, :], bank(2), ALU.mult))
                p.op("dve", reads=[bx, PSB[3]], writes=[bo],
                     emit=lambda e, cp=cp: e.tensor_tensor(
                         GO[:, cp, :], GX[:, cp, :], bank(3), ALU.subtract))
                p.op("act", reads=[bo], writes=[bb, TB("GACC")],
                     emit=lambda e, st=st, oc=oc, cp=cp: e.activation(
                         GB[:, cp, :], GO[:, cp, :], AF.Copy,
                         accum_out=GACC[:, oc, st:st + 1]))

        p.op("dve", reads=[TB("GACC")], writes=[TB("MEANT")],
             emit=lambda e: e.reduce_sum(MEANT[:, :].unsqueeze(2),
                                         GACC[:, :, :], axis=AX.X))
        p.op("dve", reads=[TB("MEANT")], writes=[TB("MEANT")],
             emit=lambda e: e.tensor_single_scalar(MEANT[:, :], MEANT[:, :],
                                                   1.0 / S, ALU.mult))
        p.op("act", reads=[TB("MEANT")], writes=[TB("MEANB")],
             emit=lambda e: e.activation(MEANB[:, :], MEANT[:, :], AF.Copy))

        def mkg1():
            def em(e):
                ins = None
                for oc in range(4):
                    for ic in range(8):
                        ins = e.matmul(PS[:, oc:oc + 1],
                                       GW1[:, ic, oc * 128:(oc + 1) * 128],
                                       MEANB[:, ic:ic + 1], start=(ic == 0),
                                       stop=(ic == 7), skip_group_check=True)
                return ins
            return em
        p.op("pe", reads=[TB("GW1"), TB("MEANB")], writes=[PSB[0]],
             emit=pe_guard(0, 128, 0, 128, mkg1()))
        for oc in range(4):
            p.op("act", reads=[PSB[0], TB("GBX1")], writes=[TB("GH")],
                 emit=lambda e, oc=oc: e.activation(
                     GH[:, oc:oc + 1], PS[:, oc:oc + 1], AF.Gelu,
                     bias=GBX[:, oc:oc + 1]))

        def mkg2():
            def em(e):
                ins = None
                for oc in range(2):
                    for ic in range(4):
                        ins = e.matmul(PS[:, 512 + oc:513 + oc],
                                       GW2[:, ic, oc * 128:(oc + 1) * 128],
                                       GH[:, ic:ic + 1], start=(ic == 0),
                                       stop=(ic == 3), skip_group_check=True)
                return ins
            return em
        p.op("pe", reads=[TB("GW2"), TB("GH")], writes=[PSB[1]],
             emit=pe_guard(0, 128, 0, 128, mkg2()))
        for oc in range(2):
            p.op("act", reads=[PSB[1], TB("GBX2")], writes=[TB("GH2")],
                 emit=lambda e, oc=oc: e.activation(
                     GH[:, 4 + oc:5 + oc], PS[:, 512 + oc:513 + oc], AF.Gelu,
                     bias=GBX[:, 4 + oc:5 + oc]))

        def mkg3():
            def em(e):
                ins = None
                for ic in range(2):
                    ins = e.matmul(PS[0:1, 1024:1025], GW3[:, ic, 0:1],
                                   GH[:, 4 + ic:5 + ic], start=(ic == 0),
                                   stop=(ic == 1), skip_group_check=True)
                return ins
            return em
        p.op("pe", reads=[TB("GW3"), TB("GH2")], writes=[PSB[2]],
             emit=pe_guard(0, 128, 0, 128, mkg3()))
        p.op("act", reads=[PSB[2], TB("SMLg3")], writes=[TB("SMLghh")],
             emit=lambda e: e.activation(SML[0:1, 2:3], PS[0:1, 1024:1025],
                                         AF.Tanh, scale=0.5,
                                         bias=SML[0:1, 1:2]))
        p.op("dve", reads=[TB("SMLghh"), TB("SMLar")], writes=[TB("SMLgt")],
             emit=lambda e: e.tensor_tensor(SML[0:1, 5:6], SML[0:1, 3:4],
                                            SML[0:1, 2:3], ALU.mult))
        p.op("dve", reads=[TB("SMLgt")], writes=[TB("SMLgt")],
             emit=lambda e: e.tensor_scalar(SML[0:1, 5:6], SML[0:1, 5:6],
                                            0.7, BASE_COUPLING, ALU.mult,
                                            ALU.add))

        def cn_for_st(st, lsl):
            def d1(oc, bk):
                p.op("act", reads=[PSB[bk], TB("CBX1")], writes=[TB("HB1")],
                     emit=lambda e, oc=oc, bk=bk: e.activation(
                         HB1[:, oc, :], bank(bk), AF.Gelu, bias=CBX[:, oc:oc + 1]))
            proj_gemm(d1, CW1, TB("CW1"), lambda ic: OUTB[:, ic, :], TB("OUTB"), 8)

            def d2(oc, bk):
                p.op("act", reads=[PSB[bk], TB("CBX2")], writes=[TB("HB2")],
                     emit=lambda e, oc=oc, bk=bk: e.activation(
                         HB2[:, oc, :], bank(bk), AF.Gelu,
                         bias=CBX[:, 8 + oc:9 + oc]))
            proj_gemm(d2, CW2, TB("CW2"), lambda ic: HB1[:, ic, :], TB("HB1"), 4)

            def d3(oc, bk):
                p.op("act", reads=[PSB[bk], TB("CBX3")], writes=[TB("HB3")],
                     emit=lambda e, oc=oc, bk=bk: e.activation(
                         HB3[:, oc, :], bank(bk), AF.Gelu,
                         bias=CBX[:, 12 + oc:13 + oc]))
            proj_gemm(d3, CW3, TB("CW3"), lambda ic: HB2[:, ic, :], TB("HB2"),
                      2, k_ic=4)

            def mk4():
                def em(e):
                    ins = None
                    for ic in range(2):
                        ins = e.matmul(PS[0:1, 0:512], CW4[:, ic, 0:1],
                                       HB3[:, ic, :], start=(ic == 0),
                                       stop=(ic == 1))
                    return ins
                return em
            p.op("pe", reads=[TB("CW4"), TB("HB3")], writes=[PSB[0]],
                 emit=pe_guard(0, 128, 0, 128, mk4()))
            p.op("act", reads=[PSB[0], TB("SMLb4")], writes=[TB("TFHs")],
                 emit=lambda e: e.activation(
                     TFH[0:1, 0, :], PS[0:1, 0:512], AF.Tanh,
                     scale=0.5, bias=SML[0:1, 0:1]))
            p.dma("sp", reads=[TB("TFHs")], writes=[TB("tfh_d")],
                  emit=lambda e, st=st: e.dma_start(
                      out=tfh_d[st, :].unsqueeze(0), in_=TFH[0:1, 0, :]))

        for hf in range(2):
            sts = list(range(hf * 4, hf * 4 + 4))
            for st in sts:
                lsl = slice((st - hf * 4) * ST, (st - hf * 4 + 1) * ST)
                emit_ln_bcast(ln_src, st, 2, 3)
                for oc in range(NOC):
                    cp = oc % 2
                    bx = B(f"GX{cp}")
                    p.dma("sp", writes=[bx],
                          emit=lambda e, st=st, oc=oc, cp=cp: e.dma_start(
                              out=GX[:, cp, :],
                              in_=x_src[oc * 128:(oc + 1) * 128,
                                        st * ST:(st + 1) * ST]))
                    p.op("dve", reads=[bx, PSB[2]], writes=[bx],
                         emit=lambda e, cp=cp: e.tensor_tensor(
                             GX[:, cp, :], GX[:, cp, :], bank(2), ALU.mult))
                    p.op("dve", reads=[bx, PSB[3]], writes=[TB("OUT")],
                         emit=lambda e, oc=oc, cp=cp, lsl=lsl: e.tensor_tensor(
                             OUT[:, oc, lsl], GX[:, cp, :], bank(3),
                             ALU.subtract))
                    p.op("act", reads=[TB("OUT")], writes=[TB("OUTB")],
                         emit=lambda e, oc=oc, lsl=lsl: e.activation(
                             OUTB[:, oc, :], OUT[:, oc, lsl], AF.Copy))
                cn_for_st(st, lsl)
            for k in range(3):
                p.dma("sp", reads=[TB("tfh_d")], writes=[TB("TFHs")],
                      emit=lambda e, hf=hf: e.dma_start(
                          out=CPL[0:1, 0, :],
                          in_=tfh_d[hf * 4:hf * 4 + 4, :].rearrange(
                              "a b -> (a b)").unsqueeze(0)))
                p.op("dve", reads=[TB("TFHs"), TB("SMLar3"), TB("SMLgt")],
                     writes=[TB("CPL")],
                     emit=lambda e: e.tensor_scalar(
                         CPL[0:1, 0, :], CPL[0:1, 0, :], SML[0:1, 4:5],
                         SML[0:1, 5:6], ALU.mult, ALU.add))
                for qb in range(4):
                    p.op("pe", reads=[TB("CPL"), B("ONESF")], writes=[PSB[2 + qb]],
                         emit=pe_guard(0, 1, 0, 1,
                                       lambda e, qb=qb: e.matmul(
                                           bank(2 + qb), ONESF[:, :],
                                           CPL[0:1, 0, qb * 512:(qb + 1) * 512],
                                           start=True, stop=True)))
                hsl = slice(0, SH)
                for pi, (a, bq) in enumerate([(0, 4), (1, 5), (2, 6), (3, 7)]):
                    alu1 = ALU.add if a < 2 else ALU.subtract
                    alu2 = ALU.subtract if a < 2 else ALU.add
                    p.op("dve", reads=[PSB[2], PSB[3], PSB[4], PSB[5], TB("OUT")],
                         writes=[TB("TMPU")],
                         emit=lambda e, bq=bq: e.tensor_tensor(
                             TMPU[:, :], bank(2, 4), OUT[:, bq, :], ALU.mult))
                    p.op("dve", reads=[PSB[2], PSB[3], PSB[4], PSB[5], TB("OUT")],
                         writes=[TB("TMPV")],
                         emit=lambda e, a=a: e.tensor_tensor(
                             TMPV[:, :], bank(2, 4), OUT[:, a, :], ALU.mult))
                    p.op("dve", reads=[TB("TMPU"), TB("OUT")], writes=[TB("OUT")],
                         emit=lambda e, a=a, alu1=alu1: e.tensor_tensor(
                             OUT[:, a, :], OUT[:, a, :], TMPU[:, :], alu1))
                    p.op("dve", reads=[TB("TMPV"), TB("OUT")], writes=[TB("OUT")],
                         emit=lambda e, bq=bq, alu2=alu2: e.tensor_tensor(
                             OUT[:, bq, :], OUT[:, bq, :], TMPV[:, :], alu2))
                for st in sts:
                    lsl = slice((st - hf * 4) * ST, (st - hf * 4 + 1) * ST)
                    if k < 2:
                        def mkcast(lsl=lsl):
                            def em(e):
                                ins = None
                                for oc in range(NOC):
                                    ins = e.activation(OUTB[:, oc, :],
                                                       OUT[:, oc, lsl], AF.Copy)
                                return ins
                            return em
                        p.op("act", reads=[TB("OUT")], writes=[TB("OUTB")],
                             emit=mkcast())
                        cn_for_st(st, lsl)
                    else:
                        for oc in range(NOC):
                            cp = oc % 2
                            bx = B(f"GR{cp}")
                            p.dma("sp", writes=[bx],
                                  emit=lambda e, st=st, oc=oc, cp=cp: e.dma_start(
                                      out=GR[:, cp, :],
                                      in_=x_src[oc * 128:(oc + 1) * 128,
                                                st * ST:(st + 1) * ST]))
                            bo = B(f"GO{cp}")
                            p.op("dve", reads=[bx, TB("OUT")], writes=[bo],
                                 emit=lambda e, oc=oc, cp=cp, lsl=lsl: e.tensor_tensor(
                                     GO[:, cp, :], OUT[:, oc, lsl],
                                     GR[:, cp, :], ALU.add))
                            p.dma("sp", reads=[bo], writes=[B(dst_buf_name)],
                                  emit=lambda e, st=st, oc=oc, cp=cp: e.dma_start(
                                      out=x_dst[oc * 128:(oc + 1) * 128,
                                                st * ST:(st + 1) * ST],
                                      in_=GO[:, cp, :]))
                            if ln_stats_out is not None:
                                bb = B(f"GB{cp}")
                                bqq = B(f"GQ{cp}")
                                p.op("act", reads=[bo], writes=[bb],
                                     emit=lambda e, cp=cp: e.activation(
                                         GB[:, cp, :], GO[:, cp, :], AF.Copy))
                                p.op("act", reads=[bb], writes=[bqq],
                                     emit=lambda e, cp=cp: e.activation(
                                         GQ[:, cp, :], GB[:, cp, :], AF.Square))
                                emit_stats_chunk(GB[:, cp, :], GQ[:, cp, :],
                                                 [bb, bqq], ln_stats_out, st, oc)
        p.barrier()
        sc.close()

    def attnb_phase():
        finalize_ln(LNI)
        scC = contextlib.ExitStack()

        def sC(nm, shape, dt):
            return scC.enter_context(nc.sbuf_tensor("C" + nm, shape, dt))

        XSSM = sC("XSSM", [128, NCC, S], BF16)
        LNIO = sC("LNIO", [128, 4, 8], F32)
        ACOL = sC("ACOL", [128, 96], F32)
        BCC = sC("BCC", [128, 96], F32)
        ABC = sC("ABC", [128, 2, NCC, D_STATE], F32)

        for i, (nm, srcd) in enumerate((("lnig", lnio_d["ig"]), ("lnib", lnio_d["ib"]),
                                        ("lnog", lnio_d["og"]), ("lnob", lnio_d["ob"]))):
            p.dma("sp", writes=[B("LNIO")],
                  emit=lambda e, i=i, s=srcd: e.dma_start(out=LNIO[:, i, :], in_=s))
        p.dma("sp", writes=[B("ABCa")],
              emit=lambda e: e.dma_start(out=ABC[:, 0, :, :], in_=alog_in))
        p.dma("sp", writes=[B("ABCb")],
              emit=lambda e: e.dma_start(out=ABC[:, 1, :, :], in_=bmat_in))
        p.op("dve", reads=[B("ABCa")], writes=[B("ACOL")],
             emit=lambda e: e.tensor_scalar(
                 ACOL[:, :], ABC[:, 0, :, :].rearrange("p a b -> p (a b)"),
                 0.0, -5.0, ALU.min, ALU.max))
        p.op("act", reads=[B("ACOL")], writes=[B("ACOL")],
             emit=lambda e: e.activation(ACOL[:, :], ACOL[:, :], AF.Exp))
        p.op("dve", reads=[B("ACOL")], writes=[B("ACOL")],
             emit=lambda e: e.tensor_scalar(ACOL[:, :], ACOL[:, :],
                                            2.0, 0.01, ALU.min, ALU.max))
        p.op("dve", reads=[B("ACOL")], writes=[B("ACOL")],
             emit=lambda e: e.tensor_single_scalar(ACOL[:, :], ACOL[:, :],
                                                   -0.9, ALU.mult))
        p.dma("sp", reads=[B("ACOL")], writes=[B("ABCa")],
              emit=lambda e: e.dma_start(out=ABC[:, 0, :, :], in_=cmat_in))
        p.op("dve", reads=[B("ABCa"), B("ABCb")], writes=[B("BCC")],
             emit=lambda e: e.tensor_tensor(
                 BCC[:, :], ABC[:, 0, :, :].rearrange("p a b -> p (a b)"),
                 ABC[:, 1, :, :].rearrange("p a b -> p (a b)"), ALU.mult))
        p.op("dve", reads=[B("BCC")], writes=[B("BCC")],
             emit=lambda e: e.tensor_single_scalar(BCC[:, :], BCC[:, :],
                                                   0.1, ALU.mult))

        scC1 = contextlib.ExitStack()
        WINB = scC1.enter_context(nc.sbuf_tensor("WINB", [128, 8, 3072], BF16))
        XNB = scC1.enter_context(nc.sbuf_tensor("XNB", [128, 8, 512], BF16))
        GTB = scC1.enter_context(nc.sbuf_tensor("GTB", [128, 2, 512], BF16))
        p.dma("pool", writes=[B("WINB")],
              emit=lambda e: e.dma_start(out=WINB[:, :, :], in_=winT))
        for st in range(NST):
            emit_ln_bcast(LNI, st, 2, 3)
            for oc in range(NOC):
                cp = oc % 2
                bx = B(f"GX{cp}")
                p.dma("sp", writes=[bx],
                      emit=lambda e, st=st, oc=oc, cp=cp: e.dma_start(
                          out=GX[:, cp, :],
                          in_=xB[oc * 128:(oc + 1) * 128, st * ST:(st + 1) * ST]))
                p.op("dve", reads=[bx, PSB[2]], writes=[bx],
                     emit=lambda e, cp=cp: e.tensor_tensor(
                         GX[:, cp, :], GX[:, cp, :], bank(2), ALU.mult))
                p.op("dve", reads=[bx, PSB[3]], writes=[bx],
                     emit=lambda e, cp=cp: e.tensor_tensor(
                         GX[:, cp, :], GX[:, cp, :], bank(3), ALU.subtract))
                p.op("dve", reads=[bx, B("LNIO")], writes=[B("XNB")],
                     emit=lambda e, oc=oc, cp=cp: e.tensor_scalar(
                         XNB[:, oc, :], GX[:, cp, :],
                         LNIO[:, 0, oc:oc + 1], LNIO[:, 1, oc:oc + 1],
                         ALU.mult, ALU.add))
            for ocp in range(24):
                bk = ocp % 2
                def mkwin(ocp=ocp, bk=bk):
                    def em(e):
                        ins = None
                        for ic in range(8):
                            ins = e.matmul(PS[:, bk * 512:bk * 512 + 512],
                                           WINB[:, ic, ocp * 128:(ocp + 1) * 128],
                                           XNB[:, ic, :], start=(ic == 0),
                                           stop=(ic == 7))
                        return ins
                    return em
                p.op("pe", reads=[B("WINB"), B("XNB")], writes=[PSB[bk]],
                     emit=pe_guard(0, 128, 0, 128, mkwin()))
                if ocp < NCC:
                    p.op("dve", reads=[PSB[bk]], writes=[B(f"XSSM{ocp}")],
                         emit=lambda e, st=st, ocp=ocp, bk=bk: e.tensor_scalar(
                             XSSM[:, ocp, st * ST:(st + 1) * ST], bank(bk),
                             5.0, -5.0, ALU.min, ALU.max))
                else:
                    cc = ocp - NCC
                    gp = cc % 2
                    p.op("dve", reads=[PSB[bk]], writes=[B(f"GTB{gp}")],
                         emit=lambda e, bk=bk, gp=gp: e.tensor_scalar(
                             GTB[:, gp, :], bank(bk), 5.0, -5.0,
                             ALU.min, ALU.max))
                    p.dma("sp", reads=[B(f"GTB{gp}")], writes=[B("gt_d")],
                          emit=lambda e, st=st, cc=cc, gp=gp: e.dma_start(
                              out=gt_d[cc * 128:(cc + 1) * 128,
                                       st * ST:(st + 1) * ST],
                              in_=GTB[:, gp, :]))
        p.barrier()
        scC1.close()

        scC2 = contextlib.ExitStack()
        WOUTB = scC2.enter_context(nc.sbuf_tensor("WOUTB", [128, NCC, HID], BF16))
        XF = scC2.enter_context(nc.sbuf_tensor("CXF", [128, S], F32))
        HT = scC2.enter_context(nc.sbuf_tensor("CHT", [128, S], F32))
        YC = scC2.enter_context(nc.sbuf_tensor("CYC", [128, S], F32))
        GTF = scC2.enter_context(nc.sbuf_tensor("CGTF", [128, S], BF16))
        BRT = scC2.enter_context(nc.sbuf_tensor("CBRT", [128, 2, 512], F32))
        p.dma("pool", writes=[B("WOUTB")],
              emit=lambda e: e.dma_start(out=WOUTB[:, :, :], in_=woutT))
        for cc in range(NCC):
            p.op("act", reads=[B(f"XSSM{cc}")], writes=[B("CXF")],
                 emit=lambda e, cc=cc: e.activation(XF[:, :], XSSM[:, cc, :],
                                                    AF.Copy))
            for s in range(D_STATE):
                lane = cc * D_STATE + s
                p.op("dve", reads=[B("CXF"), B("ACOL")], writes=[B("CHT")],
                     emit=lambda e, lane=lane: e.tensor_tensor_scan(
                         HT[:, :], ACOL[:, lane:lane + 1].broadcast_to([128, S]),
                         XF[:, :], 0.0, ALU.mult, ALU.add))
                if s == 0:
                    p.op("dve", reads=[B("CHT"), B("BCC")], writes=[B("CYC")],
                         emit=lambda e, lane=lane: e.tensor_scalar(
                             YC[:, :], HT[:, :], BCC[:, lane:lane + 1], None,
                             ALU.mult))
                else:
                    p.op("dve", reads=[B("CHT"), B("BCC"), B("CYC")],
                         writes=[B("CYC")],
                         emit=lambda e, lane=lane: e.scalar_tensor_tensor(
                             YC[:, :], HT[:, :], BCC[:, lane:lane + 1],
                             YC[:, :], ALU.mult, ALU.add))
            p.op("dve", reads=[B("CYC")], writes=[B("CYC")],
                 emit=lambda e: e.tensor_scalar(YC[:, :], YC[:, :], 5.0, -5.0,
                                                ALU.min, ALU.max))
            p.dma("sp", writes=[B("CGTF")],
                  emit=lambda e, cc=cc: e.dma_start(out=GTF[:, :],
                                                    in_=gt_d[cc * 128:(cc + 1) * 128, :]))
            p.op("act", reads=[B("CGTF")], writes=[B("CXF")],
                 emit=lambda e: e.activation(XF[:, :], GTF[:, :], AF.Tanh))
            p.op("pool", reads=[B("CXF"), B("CYC")], writes=[B(f"XSSM{cc}")],
                 emit=lambda e, cc=cc: e.tensor_tensor(
                     XSSM[:, cc, :], YC[:, :], XF[:, :], ALU.mult))
        for st in range(NST):
            for oc in range(NOC):
                bk = oc % 2
                def mkwo(st=st, oc=oc, bk=bk):
                    def em(e):
                        ins = None
                        for ic in range(NCC):
                            ins = e.matmul(PS[:, bk * 512:bk * 512 + 512],
                                           WOUTB[:, ic, oc * 128:(oc + 1) * 128],
                                           XSSM[:, ic, st * ST:(st + 1) * ST],
                                           start=(ic == 0), stop=(ic == NCC - 1))
                        return ins
                    return em
                p.op("pe",
                     reads=[B("WOUTB")] + [B(f"XSSM{ic}") for ic in range(NCC)],
                     writes=[PSB[bk]], emit=pe_guard(0, 128, 0, 128, mkwo()))
                cp = oc % 2
                p.op("dve", reads=[PSB[bk]], writes=[B(f"CBRT{cp}")],
                     emit=lambda e, bk=bk, cp=cp: e.tensor_scalar(
                         BRT[:, cp, :], bank(bk), 5.0, -5.0, ALU.min, ALU.max))
                p.dma("sp", reads=[B(f"CBRT{cp}")], writes=[B("brC")],
                      emit=lambda e, st=st, oc=oc, cp=cp: e.dma_start(
                          out=brC[oc * 128:(oc + 1) * 128, st * ST:(st + 1) * ST],
                          in_=BRT[:, cp, :]))
                bb = B(f"GB{cp}")
                bqq = B(f"GQ{cp}")
                p.op("act", reads=[B(f"CBRT{cp}")], writes=[bb],
                     emit=lambda e, cp=cp: e.activation(GB[:, cp, :],
                                                        BRT[:, cp, :], AF.Copy))
                p.op("act", reads=[bb], writes=[bqq],
                     emit=lambda e, cp=cp: e.activation(GQ[:, cp, :],
                                                        GB[:, cp, :], AF.Square))
                emit_stats_chunk(GB[:, cp, :], GQ[:, cp, :], [bb, bqq],
                                 LNO, st, oc)
        p.barrier()
        scC2.close()

        finalize_ln(LNO)
        for st in range(NST):
            emit_ln_bcast(LNO, st, 2, 3)
            for oc in range(NOC):
                cp = oc % 2
                bx = B(f"GX{cp}")
                br = B(f"GR{cp}")
                bo = B(f"GO{cp}")
                p.dma("sp", writes=[bx],
                      emit=lambda e, st=st, oc=oc, cp=cp: e.dma_start(
                          out=GX[:, cp, :],
                          in_=brC[oc * 128:(oc + 1) * 128, st * ST:(st + 1) * ST]))
                p.dma("sp", writes=[br],
                      emit=lambda e, st=st, oc=oc, cp=cp: e.dma_start(
                          out=GR[:, cp, :],
                          in_=xB[oc * 128:(oc + 1) * 128, st * ST:(st + 1) * ST]))
                p.op("dve", reads=[bx, PSB[2]], writes=[bx],
                     emit=lambda e, cp=cp: e.tensor_tensor(
                         GX[:, cp, :], GX[:, cp, :], bank(2), ALU.mult))
                p.op("dve", reads=[bx, PSB[3]], writes=[bx],
                     emit=lambda e, cp=cp: e.tensor_tensor(
                         GX[:, cp, :], GX[:, cp, :], bank(3), ALU.subtract))
                p.op("dve", reads=[bx, B("LNIO")], writes=[bo],
                     emit=lambda e, oc=oc, cp=cp: e.tensor_scalar(
                         GO[:, cp, :], GX[:, cp, :],
                         LNIO[:, 2, oc:oc + 1], LNIO[:, 3, oc:oc + 1],
                         ALU.mult, ALU.add))
                p.op("dve", reads=[bo, br], writes=[bo],
                     emit=lambda e, cp=cp: e.tensor_tensor(
                         GO[:, cp, :], GO[:, cp, :], GR[:, cp, :], ALU.add))
                p.dma("sp", reads=[bo], writes=[B("xC")],
                      emit=lambda e, st=st, oc=oc, cp=cp: e.dma_start(
                          out=xC[oc * 128:(oc + 1) * 128, st * ST:(st + 1) * ST],
                          in_=GO[:, cp, :]))
                bb = B(f"GB{cp}")
                bqq = B(f"GQ{cp}")
                p.op("act", reads=[bo], writes=[bb],
                     emit=lambda e, cp=cp: e.activation(GB[:, cp, :],
                                                        GO[:, cp, :], AF.Copy))
                p.op("act", reads=[bb], writes=[bqq],
                     emit=lambda e, cp=cp: e.activation(GQ[:, cp, :],
                                                        GB[:, cp, :], AF.Square))
                emit_stats_chunk(GB[:, cp, :], GQ[:, cp, :], [bb, bqq],
                                 LN4, st, oc)
        p.barrier()
        scC.close()

    if phases >= 2:
        ffn_phase("B", xA, xB, LN2, LNI, "xB")
    if phases >= 3:
        attnb_phase()
    if phases >= 4:
        ffn_phase("D", xC, y_out, LN4, None, "y")

    if phases < 4:
        srcmap = {1: xA, 2: xB, 3: xC}
        xs = srcmap[phases]
        for st2 in range(NST):
            for oc in range(NOC):
                cp = oc % 2
                p.dma("sp", reads=[B("xA"), B("xB"), B("xC")], writes=[B(f"GX{cp}")],
                      emit=lambda e, st2=st2, oc=oc, cp=cp, xs=xs: e.dma_start(
                          out=GX[:, cp, :], in_=xs[oc * 128:(oc + 1) * 128,
                                                   st2 * ST:(st2 + 1) * ST]))
                p.dma("sp", reads=[B(f"GX{cp}")], writes=[B("y")],
                      emit=lambda e, st2=st2, oc=oc, cp=cp: e.dma_start(
                          out=y_out[oc * 128:(oc + 1) * 128,
                                    st2 * ST:(st2 + 1) * ST], in_=GX[:, cp, :]))

    fin = p.scounts.get("d_y", 0)
    p.raw("sp", lambda e, _c=fin: e.wait_ge(p.sems["d_y"], _c))
    p.finalize()
    return nc



def _chunk_lhsT(w):
    K, M = w.shape
    return np.ascontiguousarray(w.reshape(K // 128, 128, M).transpose(1, 0, 2))


def _prep_shared(inp):
    f = np.float32

    def arr(k):
        return np.asarray(inp[k], f)

    d = {}
    d["wqT"] = _chunk_lhsT(np.ascontiguousarray(arr("L0_attn_a_w_q").T))
    d["wkT"] = _chunk_lhsT(np.ascontiguousarray(arr("L0_attn_a_w_k").T))
    d["wvT"] = _chunk_lhsT(np.ascontiguousarray(arr("L0_attn_a_w_v").T))
    d["wrelT"] = _chunk_lhsT(np.ascontiguousarray(arr("L0_attn_a_w_rel").T))
    d["wpostT"] = _chunk_lhsT(np.ascontiguousarray(arr("L0_attn_a_w_post").T))
    d["posT"] = _chunk_lhsT(np.ascontiguousarray(
        arr("L0_attn_a_position_embeddings").T))
    d["pds2"] = np.ascontiguousarray(
        np.tile(arr("L0_attn_a_per_dim_scale"), 2).reshape(128, 1))
    for k in range(4):
        d[f"cn_w{k + 1}"] = _chunk_lhsT(arr(f"L0_ffn_cn_w{k + 1}"))
        b = arr(f"L0_ffn_cn_b{k + 1}")
        d[f"cn_b{k + 1}"] = (b.reshape(1, 1) if b.size == 1
                             else np.ascontiguousarray(b.reshape(-1, 128).T))
    for k in range(3):
        d[f"gc_w{k + 1}"] = _chunk_lhsT(arr(f"L0_ffn_gc_w{k + 1}"))
        b = arr(f"L0_ffn_gc_b{k + 1}")
        d[f"gc_b{k + 1}"] = (b.reshape(1, 1) if b.size == 1
                             else np.ascontiguousarray(b.reshape(-1, 128).T))
    d["ar"] = arr("L0_ffn_adaptive_range").reshape(1, 1)
    d["winT"] = _chunk_lhsT(np.ascontiguousarray(arr("L0_attn_b_W_in").T))
    d["woutT"] = _chunk_lhsT(np.ascontiguousarray(arr("L0_attn_b_W_out").T))
    for nm, key in (("alog", "L0_attn_b_A_log"), ("bmat", "L0_attn_b_B"),
                    ("cmat", "L0_attn_b_C")):
        d[nm] = np.ascontiguousarray(
            arr(key).reshape(NCC, 128, D_STATE).transpose(1, 0, 2))
    for nm, key in (("ln_in_g", "L0_attn_b_ln_in_g"),
                    ("ln_in_b", "L0_attn_b_ln_in_b"),
                    ("ln_out_g", "L0_attn_b_ln_out_g"),
                    ("ln_out_b", "L0_attn_b_ln_out_b")):
        d[nm] = np.ascontiguousarray(arr(key).reshape(8, 128).T)
    return d


_PHASES = 4
_ABLATE = 99
_SUB = 99
_SUBB = 99


def kernel(**inputs):
    shared = _prep_shared(inputs)
    x = np.asarray(inputs["L0_attn_a_hidden_states"], np.float32)
    nc = build_core(phases=_PHASES, ablate=_ABLATE, sub=_SUB, subb=_SUBB)
    in_maps = []
    for b in range(8):
        m = dict(shared)
        m["x_fm"] = np.ascontiguousarray(x[b].T)
        in_maps.append(m)
    res = run_bass_kernel_spmd(nc, in_maps, core_ids=list(range(8)))
    out = np.stack([np.asarray(r["y_fm"]).T for r in res.results], axis=0)
    return out.astype(np.float32)
